# version 11
# baseline (speedup 1.0000x reference)
import numpy as np
import concourse.bass as bass
import concourse.mybir as mybir
from concourse.bass_utils import run_bass_kernel_spmd
from contextlib import ExitStack

F32 = mybir.dt.float32
AF = mybir.ActivationFunctionType
OP = mybir.AluOpType
AX = mybir.AxisListType

NCORES = 8
TC = 2048
TT = 256
NT = TC // TT
DM = 1024
FCH = DM // 128
DI = 2048
ICH = DI // 128
NS = 16
C1 = TT + 4
CV = TT + 1
EPS = 1e-5


class Sched:
    ENGS = ("sync", "gpsimd", "tensor", "vector", "scalar")

    def __init__(self):
        self.q = {e: [] for e in self.ENGS}
        self.wr = {}
        self.rd = {}
        self.seq = {"__g": 0}

    def emit(self, eng, fn, reads=(), writes=(), dma=False, async_=False,
             sem=None):
        deps = {}
        for r in reads:
            w = self.wr.get(r)
            if w is not None and (w["eng"] != eng or w["async"]):
                deps[id(w)] = w
        for t in writes:
            w = self.wr.get(t)
            if w is not None and (w["eng"] != eng or w["async"]):
                deps[id(w)] = w
            for rr in self.rd.get(t, ()):
                if rr["eng"] != eng or rr["async"]:
                    deps[id(rr)] = rr
        best = {}
        for d in deps.values():
            b = best.get(d["sem"])
            if b is None or d["gseq"] > b["gseq"]:
                best[d["sem"]] = d
        it = {"eng": eng, "fn": fn, "inc": 16 if dma else (1 if async_ else 0),
              "async": dma or async_, "sem": sem or eng,
              "gseq": self.seq["__g"]}
        self.seq["__g"] = self.seq["__g"] + 1
        for d in best.values():
            if d is it:
                continue
            if d["inc"] == 0:
                d["inc"] = 1
            self.q[eng].append({"wait": d})
        self.q[eng].append(it)
        for r in reads:
            self.rd.setdefault(r, []).append(it)
        for t in writes:
            self.wr[t] = it
            self.rd[t] = []
        return it

    def all_sems(self):
        names = set()
        for e in self.ENGS:
            for it in self.q[e]:
                if "wait" not in it:
                    names.add(it["sem"])
        return sorted(names)

    def finalize(self):
        cnt = {}
        for e in self.ENGS:
            for it in self.q[e]:
                if "wait" not in it:
                    it["ord"] = None
        allitems = []
        for e in self.ENGS:
            allitems.extend(it for it in self.q[e] if "wait" not in it)
        allitems.sort(key=lambda it: it["gseq"])
        for it in allitems:
            c = cnt.get(it["sem"], 0) + it["inc"]
            cnt[it["sem"]] = c
            it["cnt"] = c

    def body(self, eng, sems):
        items = self.q[eng]

        def run(px):
            hi = {}
            for it in items:
                if "wait" in it:
                    d = it["wait"]
                    v = d["cnt"]
                    ts = d["sem"]
                    if v > hi.get(ts, 0):
                        hi[ts] = v
                        px.wait_ge(sems[ts], v)
                else:
                    h = it["fn"](px)
                    if it["inc"]:
                        h.then_inc(sems[it["sem"]], it["inc"])
        return run


def build_nc():
    nc = bass.Bass("TRN2", target_bir_lowering=False, debug=False,
                   detect_race_conditions=False)

    xin = nc.dram_tensor("xin", [DM, TC + 4], F32, kind="ExternalInput").ap()
    w_in = nc.dram_tensor("w_in", [DM, 2 * DI], F32, kind="ExternalInput").ap()
    w_out = nc.dram_tensor("w_out", [DI, DM], F32, kind="ExternalInput").ap()
    wrT = nc.dram_tensor("wrT", [DM, DM], F32, kind="ExternalInput").ap()
    wkT = nc.dram_tensor("wkT", [DM, DM], F32, kind="ExternalInput").ap()
    wvT = nc.dram_tensor("wvT", [DM, DM], F32, kind="ExternalInput").ap()
    wxp = nc.dram_tensor("wxp", [128, ICH * 33], F32, kind="ExternalInput").ap()
    dpk = nc.dram_tensor("dpk", [128, ICH], F32, kind="ExternalInput").ap()
    cbk = nc.dram_tensor("cbk", [128, ICH], F32, kind="ExternalInput").ap()
    cwk = nc.dram_tensor("cwk", [128, ICH * 4], F32, kind="ExternalInput").ap()
    murp = nc.dram_tensor("murp", [128, FCH], F32, kind="ExternalInput").ap()
    mukp = nc.dram_tensor("mukp", [128, FCH], F32, kind="ExternalInput").ap()
    alog = nc.dram_tensor("alog", [NS, 1], F32, kind="ExternalInput").ap()
    sel8 = nc.dram_tensor("sel8", [NS, NCORES], F32, kind="ExternalInput").ap()
    mfirst = nc.dram_tensor("mfirst", [128, 1], F32, kind="ExternalInput").ap()
    outT = nc.dram_tensor("outT", [DM, TC], F32, kind="ExternalOutput").ap()
    dbg = nc.dram_tensor("dbg", [NS, 36], F32, kind="ExternalOutput").ap()

    convsp = nc.dram_tensor("convsp", [NT, 128, ICH * CV], F32)
    agin = nc.dram_tensor("agin", [1, 2 * NS], F32)
    agout = nc.dram_tensor("agout", [NCORES, 2 * NS], F32)

    xin_r = xin.rearrange("(f p) t -> p f t", p=128)
    outT_r = outT.rearrange("(f p) t -> p f t", p=128)

    S = Sched()
    with ExitStack() as ctx:
        E = ctx.enter_context
        x0t = [E(nc.sbuf_tensor(f"x0t{i}", [128, FCH * C1], F32)) for i in (0, 1)]
        convin = E(nc.sbuf_tensor("convin", [128, ICH * C1], F32))
        convt = [E(nc.sbuf_tensor(f"convt{i}", [128, ICH * CV], F32)) for i in (0, 1)]
        sg = [E(nc.sbuf_tensor(f"sg{i}", [128, ICH * CV], F32)) for i in (0, 1)]
        accA = E(nc.sbuf_tensor("accA", [128, CV], F32))
        accB = E(nc.sbuf_tensor("accB", [128, CV], F32))
        xdbl = E(nc.sbuf_tensor("xdbl", [33, CV], F32))
        bcs = E(nc.sbuf_tensor("bcs", [NS, 2 * CV], F32))
        d16 = E(nc.sbuf_tensor("d16", [NS, CV], F32))
        dAe = E(nc.sbuf_tensor("dAe", [NS, CV], F32))
        dBx = E(nc.sbuf_tensor("dBx", [NS, CV], F32))
        zer = E(nc.sbuf_tensor("zer", [NS, CV], F32))
        gb = E(nc.sbuf_tensor("gb", [NS, CV], F32))
        pb = E(nc.sbuf_tensor("pb", [NS, CV], F32))
        hb = E(nc.sbuf_tensor("hb", [NS, CV], F32))
        ycb = E(nc.sbuf_tensor("ycb", [NS, CV], F32))
        mtb = E(nc.sbuf_tensor("mtb", [NS, CV], F32))
        Mall = E(nc.sbuf_tensor("Mall", [NS, NT * CV], F32))
        ylall = E(nc.sbuf_tensor("ylall", [1, NT * CV], F32))
        Lb = E(nc.sbuf_tensor("Lb", [NS, NT + 1], F32))
        Fb = E(nc.sbuf_tensor("Fb", [NS, NT + 1], F32))
        geb = E(nc.sbuf_tensor("geb", [NS, 2], F32))
        gth = E(nc.sbuf_tensor("gth", [NS, 2 * NCORES], F32))
        hall = E(nc.sbuf_tensor("hall", [NS, NCORES], F32))
        hsel = E(nc.sbuf_tensor("hsel", [NS, NCORES], F32))
        hin = E(nc.sbuf_tensor("hin", [NS, 1], F32))
        dtm = E(nc.sbuf_tensor("dtm", [1, CV], F32))
        dsb = E(nc.sbuf_tensor("dsb", [1, CV], F32))
        expa = E(nc.sbuf_tensor("expa", [NS, 1], F32))
        nega = E(nc.sbuf_tensor("nega", [NS, 1], F32))
        resid = E(nc.sbuf_tensor("resid", [128, FCH * CV], F32))
        sqb = E(nc.sbuf_tensor("sqb", [128, FCH * CV], F32))
        x1t = E(nc.sbuf_tensor("x1t", [128, FCH * CV], F32))
        ysb = E(nc.sbuf_tensor("ysb", [1, CV], F32))
        mean = E(nc.sbuf_tensor("mean", [1, CV], F32))
        msq = E(nc.sbuf_tensor("msq", [1, CV], F32))
        var = E(nc.sbuf_tensor("var", [1, CV], F32))
        sdv = E(nc.sbuf_tensor("sdv", [1, CV], F32))
        rstd = E(nc.sbuf_tensor("rstd", [1, CV], F32))
        dbuf = E(nc.sbuf_tensor("dbuf", [128, FCH * TT], F32))
        xkb = E(nc.sbuf_tensor("xkb", [128, FCH * TT], F32))
        rbuf = E(nc.sbuf_tensor("rbuf", [128, FCH * TT], F32))
        NWB = 12
        w_ab = [E(nc.sbuf_tensor(f"w_ab{i}", [128, 384], F32)) for i in range(NWB)]
        wx_s = E(nc.sbuf_tensor("wx_s", [128, ICH * 33], F32))
        dpk_s = E(nc.sbuf_tensor("dpk_s", [128, ICH], F32))
        cbk_s = E(nc.sbuf_tensor("cbk_s", [128, ICH], F32))
        cwk_s = E(nc.sbuf_tensor("cwk_s", [128, ICH * 4], F32))
        mur_s = E(nc.sbuf_tensor("mur_s", [128, FCH], F32))
        muk_s = E(nc.sbuf_tensor("muk_s", [128, FCH], F32))
        alog_s = E(nc.sbuf_tensor("alog_s", [NS, 1], F32))
        sel_s = E(nc.sbuf_tensor("sel_s", [NS, NCORES], F32))
        mf_s = E(nc.sbuf_tensor("mf_s", [128, 1], F32))
        ones128 = E(nc.sbuf_tensor("ones128", [128, 1], F32))
        o1x128 = E(nc.sbuf_tensor("o1x128", [1, 128], F32))
        ones16 = E(nc.sbuf_tensor("ones16", [NS, 1], F32))
        o1x16 = E(nc.sbuf_tensor("o1x16", [1, NS], F32))
        ps = [E(nc.psum_tensor(f"ps{i}", [128, 512], F32)) for i in range(8)]

        def dma(eng, out, in_, reads, writes, slow=False):
            return S.emit(
                eng,
                lambda px, out=out, in_=in_, slow=slow: px.dma_start(
                    out=out, in_=in_, allow_slow_non_contiguous=slow),
                reads=reads, writes=writes, dma=True, sem="d_" + writes[0])

        wpar = [0]
        rpar = [0]

        def mm_phase(wap_fn, kcount, rhs_fn, rres, ochunks, ncols, evac_fn,
                     post_round=None):
            o0 = 0
            while o0 < ochunks:
                o1 = min(o0 + 3, ochunks)
                base = 3 * rpar[0]
                rpar[0] ^= 1
                for kc in range(kcount):
                    wb = w_ab[wpar[0]]
                    wres = f"w_ab{wpar[0]}"
                    weng = ("sync", "gpsimd", "scalar")[wpar[0] % 3]
                    wpar[0] = (wpar[0] + 1) % NWB
                    nw = (o1 - o0) * 128
                    dma(weng, wb[:, 0:nw], wap_fn(kc, o0 * 128, o1 * 128),
                        (), (wres,))
                    rap = rhs_fn(kc)
                    for i in range(o1 - o0):
                        b = base + i
                        S.emit("tensor", lambda px, wb=wb, i=i, kc=kc, b=b,
                               rap=rap, nc_=ncols: px.matmul(
                                   ps[b][:, :nc_], wb[:, i * 128:(i + 1) * 128],
                                   rap, start=(kc == 0), stop=(kc == kcount - 1)),
                               reads=(wres, rres), writes=(f"ps{b}",))
                if post_round is not None:
                    post_round(o0, o1)
                for i in range(o1 - o0):
                    evac_fn(o0 + i, ps[base + i], f"ps{base + i}")
                o0 = o1

        dma("sync", wx_s[:, :], wxp[:, :], (), ("wx",))
        dma("sync", dpk_s[:, :], dpk[:, :], (), ("dpk",))
        dma("sync", cbk_s[:, :], cbk[:, :], (), ("cbk",))
        dma("sync", cwk_s[:, :], cwk[:, :], (), ("cwk",))
        dma("sync", mur_s[:, :], murp[:, :], (), ("mur",))
        dma("sync", muk_s[:, :], mukp[:, :], (), ("muk",))
        dma("sync", alog_s[:, :], alog[:, :], (), ("alog",))
        dma("sync", sel_s[:, :], sel8[:, :], (), ("sel",))
        dma("sync", mf_s[:, :], mfirst[:, :], (), ("mf",))
        for buf, val, res in ((ones128, 1.0, "ones"), (o1x128, 1.0, "ones"),
                              (ones16, 1.0, "ones"), (o1x16, 1.0, "ones"),
                              (zer, 0.0, "zer")):
            S.emit("vector", lambda px, b=buf, v=val: px.memset(b[:, :], v),
                   reads=(), writes=(res,))
        S.emit("vector", lambda px: px.memset(Lb[:, 0:1], 0.0), (), ("Lb",))
        S.emit("vector", lambda px: px.memset(Fb[:, 0:1], 1.0), (), ("Fb",))
        S.emit("scalar", lambda px: px.activation(expa[:, :], alog_s[:, :], AF.Exp),
               reads=("alog",), writes=("expa",))
        S.emit("vector", lambda px: px.tensor_scalar(
            nega[:, :], expa[:, :], -1.0, None, OP.mult),
            reads=("expa",), writes=("nega",))

        def ln_block(src, sres, dst, dres, cols):
            for f in range(FCH):
                S.emit("scalar", lambda px, f=f: px.activation(
                    sqb[:, f * CV:f * CV + cols],
                    src[:, f * cols:f * cols + cols], AF.Square),
                    reads=(sres,), writes=("sqb",))
            for f in range(FCH):
                S.emit("tensor", lambda px, f=f: px.matmul(
                    ps[6][0:1, :cols], ones128[:, :],
                    src[:, f * cols:f * cols + cols],
                    start=(f == 0), stop=(f == FCH - 1)),
                    reads=(sres, "ones"), writes=("ps6",))
            S.emit("vector", lambda px: px.tensor_scalar(
                mean[:, :cols], ps[6][0:1, :cols], 1.0 / DM, None, OP.mult),
                reads=("ps6",), writes=("mean",))
            for f in range(FCH):
                S.emit("tensor", lambda px, f=f: px.matmul(
                    ps[6][0:1, :cols], ones128[:, :],
                    sqb[:, f * CV:f * CV + cols],
                    start=(f == 0), stop=(f == FCH - 1)),
                    reads=("sqb", "ones"), writes=("ps6",))
            S.emit("vector", lambda px: px.tensor_tensor(
                msq[:, :cols], mean[:, :cols], mean[:, :cols], OP.mult),
                reads=("mean",), writes=("msq",))
            S.emit("vector", lambda px: px.tensor_scalar(
                msq[:, :cols], msq[:, :cols], EPS, None, OP.subtract),
                reads=("msq",), writes=("msq",))
            S.emit("vector", lambda px: px.scalar_tensor_tensor(
                var[:, :cols], ps[6][0:1, :cols], 1.0 / DM, msq[:, :cols],
                OP.mult, OP.subtract),
                reads=("ps6", "msq"), writes=("var",))
            S.emit("scalar", lambda px: px.activation(
                sdv[:, :cols], var[:, :cols], AF.Sqrt),
                reads=("var",), writes=("sdv",))
            S.emit("vector", lambda px: px.reciprocal(
                rstd[:, :cols], sdv[:, :cols]),
                reads=("sdv",), writes=("rstd",))
            S.emit("tensor", lambda px: px.matmul(
                ps[6][:, :cols], o1x128[:, :], mean[:, :cols],
                start=True, stop=True),
                reads=("mean", "ones"), writes=("ps6",))
            for f in range(FCH):
                S.emit("vector", lambda px, f=f: px.tensor_tensor(
                    dst[:, f * cols:f * cols + cols],
                    src[:, f * cols:f * cols + cols],
                    ps[6][:, :cols], OP.subtract),
                    reads=(sres, "ps6"), writes=(dres,))
            S.emit("tensor", lambda px: px.matmul(
                ps[6][:, :cols], o1x128[:, :], rstd[:, :cols],
                start=True, stop=True),
                reads=("rstd", "ones"), writes=("ps6",))
            for f in range(FCH):
                S.emit("vector", lambda px, f=f: px.tensor_tensor(
                    dst[:, f * cols:f * cols + cols],
                    dst[:, f * cols:f * cols + cols],
                    ps[6][:, :cols], OP.mult),
                    reads=(dres, "ps6"), writes=(dres,))

        def conv_chunk(j, c):
            p = j % 2
            ct = convt[p]
            ctres = f"convt{p}"
            bi = c * C1
            acc = accA if c % 2 == 0 else accB
            acr = "accA" if c % 2 == 0 else "accB"
            S.emit("vector", lambda px, c=c, bi=bi, acc=acc: px.tensor_scalar(
                acc[:, :CV], convin[:, bi:bi + CV],
                cwk_s[:, c * 4:c * 4 + 1], cbk_s[:, c:c + 1],
                OP.mult, OP.add),
                reads=("convin", "cwk", "cbk"), writes=(acr,))
            for k in range(1, 4):
                S.emit("vector", lambda px, c=c, bi=bi, k=k, acc=acc:
                       px.scalar_tensor_tensor(
                           acc[:, :CV], convin[:, bi + k:bi + k + CV],
                           cwk_s[:, c * 4 + k:c * 4 + k + 1], acc[:, :CV],
                           OP.mult, OP.add),
                       reads=("convin", "cwk", acr), writes=(acr,))
            S.emit("scalar", lambda px, c=c, acc=acc, ct=ct: px.activation(
                ct[:, c * CV:(c + 1) * CV], acc[:, :CV], AF.Silu),
                reads=(acr,), writes=(ctres,))
            if c == ICH - 1:
                dma("gpsimd", convsp[j, :, :], ct[:, :], (ctres,), ("spill",))

        def s1_load(j):
            p = j % 2
            dma("sync", x0t[p].rearrange("p (f t) -> p f t", f=FCH),
                xin_r[:, :, j * TT:j * TT + C1], (), (f"x0t{p}",))

        def s1_xzh1(j):
            p = j % 2
            x0 = x0t[p]
            xres = f"x0t{p}"

            def post(o0, o1):
                if j >= 1:
                    for c in range(o0, o1):
                        conv_chunk(j - 1, c)

            def evac(o, pst, bres):
                S.emit("vector", lambda px, o=o, pst=pst: px.tensor_copy(
                    convin[:, o * C1:(o + 1) * C1], pst[:, :C1]),
                    reads=(bres,), writes=("convin",))
            mm_phase(lambda kc, c0, c1: w_in[kc * 128:(kc + 1) * 128, c0:c1],
                     FCH, lambda kc: x0[:, kc * C1:(kc + 1) * C1], xres,
                     ICH, C1, evac, post_round=post)

        def s1_wx(j):
            p = j % 2
            ct = convt[p]
            ctres = f"convt{p}"
            for kc in range(ICH):
                S.emit("tensor", lambda px, kc=kc, ct=ct: px.matmul(
                    ps[6][0:33, :CV], wx_s[:, kc * 33:(kc + 1) * 33],
                    ct[:, kc * CV:(kc + 1) * CV],
                    start=(kc == 0), stop=(kc == ICH - 1)),
                    reads=("wx", ctres), writes=("ps6",))
            S.emit("vector", lambda px: px.tensor_copy(xdbl[:, :], ps[6][0:33, :CV]),
                   reads=("ps6",), writes=("xdbl",))
            dma("gpsimd", bcs[:, 0:CV], xdbl[1:17, :], ("xdbl",), ("bcs",))
            dma("gpsimd", bcs[:, CV:2 * CV], xdbl[17:33, :], ("xdbl",), ("bcs",))
            S.emit("scalar", lambda px: px.activation(dtm[:, :], xdbl[0:1, :], AF.Exp),
                   reads=("xdbl",), writes=("dtm",))
            S.emit("scalar", lambda px: px.activation(dsb[:, :], dtm[:, :], AF.Ln,
                                                      bias=1.0),
                   reads=("dtm",), writes=("dsb",))
            S.emit("tensor", lambda px: px.matmul(
                ps[7][0:NS, :CV], o1x16[:, :], dsb[:, :], start=True, stop=True),
                reads=("dsb", "ones"), writes=("ps7",))
            S.emit("vector", lambda px: px.tensor_copy(d16[:, :], ps[7][0:NS, :CV]),
                   reads=("ps7",), writes=("d16",))

        def s1_scan(j):
            p = j % 2
            ct = convt[p]
            ctres = f"convt{p}"
            S.emit("vector", lambda px: px.tensor_scalar(
                dAe[:, :], d16[:, :], nega[:, 0:1], None, OP.mult),
                reads=("d16", "nega"), writes=("dAe",))
            S.emit("scalar", lambda px: px.activation(dAe[:, :], dAe[:, :], AF.Exp),
                   reads=("dAe",), writes=("dAe",))
            S.emit("vector", lambda px, ct=ct: px.tensor_tensor(
                dBx[:, :], bcs[:, 0:CV], ct[0:NS, 0:CV], OP.mult),
                reads=("bcs", ctres), writes=("dBx",))
            S.emit("vector", lambda px: px.tensor_tensor(
                dBx[:, :], dBx[:, :], d16[:, :], OP.mult),
                reads=("dBx", "d16"), writes=("dBx",))
            S.emit("vector", lambda px: px.memset(dAe[:, 0:1], 1.0), ("dAe",), ("dAe",))
            S.emit("vector", lambda px: px.memset(dBx[:, 0:1], 0.0), ("dBx",), ("dBx",))
            S.emit("vector", lambda px: px.tensor_tensor_scan(
                gb[:, :], dAe[:, :], dBx[:, :], 0.0, OP.mult, OP.add),
                reads=("dAe", "dBx"), writes=("gb",))
            S.emit("vector", lambda px: px.tensor_tensor_scan(
                pb[:, :], dAe[:, :], zer[:, :], 1.0, OP.mult, OP.add),
                reads=("dAe", "zer"), writes=("pb",))
            S.emit("vector", lambda px, j=j: px.scalar_tensor_tensor(
                hb[:, :], pb[:, :], Lb[:, j:j + 1], gb[:, :], OP.mult, OP.add),
                reads=("pb", "Lb", "gb"), writes=("hb",))
            S.emit("vector", lambda px: px.tensor_tensor(
                ycb[:, :], hb[:, :], bcs[:, CV:2 * CV], OP.mult),
                reads=("hb", "bcs"), writes=("ycb",))
            S.emit("vector", lambda px: px.tensor_tensor(
                mtb[:, :], bcs[:, CV:2 * CV], pb[:, :], OP.mult),
                reads=("bcs", "pb"), writes=("mtb",))
            S.emit("vector", lambda px, j=j: px.tensor_scalar(
                Mall[:, j * CV:(j + 1) * CV], mtb[:, :], Fb[:, j:j + 1], None,
                OP.mult),
                reads=("mtb", "Fb"), writes=("Mall",))
            S.emit("vector", lambda px, j=j: px.scalar_tensor_tensor(
                Lb[:, j + 1:j + 2], pb[:, CV - 1:CV], Lb[:, j:j + 1],
                gb[:, CV - 1:CV], OP.mult, OP.add),
                reads=("pb", "gb", "Lb"), writes=("Lb",))
            S.emit("vector", lambda px, j=j: px.tensor_tensor(
                Fb[:, j + 1:j + 2], pb[:, CV - 1:CV], Fb[:, j:j + 1], OP.mult),
                reads=("pb", "Fb"), writes=("Fb",))

        def s1_yloc(j):
            S.emit("tensor", lambda px: px.matmul(
                ps[7][0:1, :CV], ones16[:, :], ycb[:, :], start=True, stop=True),
                reads=("ycb", "ones"), writes=("ps7",))
            S.emit("vector", lambda px, j=j: px.tensor_copy(
                ylall[:, j * CV:(j + 1) * CV], ps[7][0:1, :CV]),
                reads=("ps7",), writes=("ylall",))

        s1_load(0)
        for j in range(NT + 2):
            if j < NT:
                s1_xzh1(j)
                if j + 1 < NT:
                    s1_load(j + 1)
            elif j == NT:
                for c in range(ICH):
                    conv_chunk(NT - 1, c)
            if 2 <= j:
                s1_yloc(j - 2)
            if 1 <= j <= NT:
                s1_wx(j - 1)
                s1_scan(j - 1)

        def emit_collective():
            S.emit("vector", lambda px: px.tensor_copy(geb[:, 0:1], Lb[:, NT:NT + 1]),
                   reads=("Lb",), writes=("geb",))
            S.emit("vector", lambda px: px.tensor_copy(geb[:, 1:2], Fb[:, NT:NT + 1]),
                   reads=("Fb",), writes=("geb",))
            dma("gpsimd", agin[:, :], geb[:, :], ("geb",), ("agin",))
            S.emit("gpsimd", lambda px: px.collective_compute(
                "AllGather", OP.bypass, replica_groups=[list(range(NCORES))],
                ins=[agin.ap()], outs=[agout.ap()]),
                reads=("agin",), writes=("agout",), async_=True)
            dma("gpsimd", gth[:, :].rearrange("p (c x) -> p c x", x=2),
                agout.ap().rearrange("c (i x) -> i c x", x=2),
                ("agout",), ("gth",), slow=True)
            def vself(it):
                if it["inc"] == 0:
                    it["inc"] = 1
                S.q["vector"].append({"wait": it})
            vself(S.emit("vector",
                         lambda px: px.tensor_copy(hall[:, 0:1], gth[:, 0:1]),
                         reads=("gth",), writes=("hall",)))
            for c in range(1, NCORES):
                vself(S.emit("vector", lambda px, c=c: px.scalar_tensor_tensor(
                    hall[:, c:c + 1], gth[:, 2 * c + 1:2 * c + 2],
                    hall[:, c - 1:c], gth[:, 2 * c:2 * c + 1], OP.mult, OP.add),
                    reads=("gth", "hall"), writes=("hall",)))
            vself(S.emit("vector", lambda px: px.tensor_tensor(
                hsel[:, :], hall[:, :], sel_s[:, :], OP.mult),
                reads=("hall", "sel"), writes=("hsel",)))
            vself(S.emit("vector", lambda px: px.tensor_tensor(
                hin[:, :], hsel[:, 0:1], hsel[:, 1:2], OP.add),
                reads=("hsel",), writes=("hin",)))
            for c in range(2, NCORES):
                vself(S.emit("vector", lambda px, c=c: px.scalar_tensor_tensor(
                    hin[:, :], hsel[:, c:c + 1], 1.0, hin[:, :], OP.mult, OP.add),
                    reads=("hsel", "hin"), writes=("hin",)))
            dma("gpsimd", dbg[:, 0:2], geb[:, :], ("geb",), ("dbg",), slow=True)
            dma("gpsimd", dbg[:, 2:18], gth[:, :], ("gth",), ("dbg",), slow=True)
            dma("gpsimd", dbg[:, 18:26], hall[:, :], ("hall",), ("dbg",), slow=True)
            dma("gpsimd", dbg[:, 26:27], hin[:, :], ("hin",), ("dbg",), slow=True)
            dma("gpsimd", dbg[:, 27:35], hsel[:, :], ("hsel",), ("dbg",), slow=True)

        def s2_load(j):
            p = j % 2
            base2 = j * TT + 3
            dma("gpsimd", convt[p][:, :], convsp[j, :, :], ("spill",),
                (f"convt{p}",))
            dma("sync",
                x0t[p][:, 0:FCH * CV].rearrange("p (f t) -> p f t", f=FCH),
                xin_r[:, :, base2:base2 + CV], (), (f"x0t{p}",))

        def s2_front_a(j):
            p = j % 2
            ct = convt[p]
            ctres = f"convt{p}"
            x0 = x0t[p]
            xres = f"x0t{p}"
            if j <= 1:
                s2_load(j)

            sgb = sg[p]

            def evac(o, pst, bres):
                S.emit("scalar", lambda px, o=o, pst=pst, sgb=sgb: px.activation(
                    sgb[:, o * CV:(o + 1) * CV], pst[:, :CV], AF.Silu),
                    reads=(bres,), writes=(f"sg{p}",))
            mm_phase(lambda kc, c0, c1: w_in[kc * 128:(kc + 1) * 128,
                                             DI + c0:DI + c1],
                     FCH, lambda kc, x0=x0: x0[:, kc * CV:(kc + 1) * CV], xres,
                     ICH, CV, evac)

        def s2_front_b(j):
            p = j % 2
            ct = convt[p]
            ctres = f"convt{p}"
            S.emit("tensor", lambda px, j=j: px.matmul(
                ps[7][0:1, :CV], hin[:, :], Mall[:, j * CV:(j + 1) * CV],
                start=True, stop=True),
                reads=("hin", "Mall"), writes=("ps7",))
            S.emit("vector", lambda px, j=j: px.tensor_tensor(
                ysb[:, :], ps[7][0:1, :CV], ylall[:, j * CV:(j + 1) * CV], OP.add),
                reads=("ps7", "ylall"), writes=("ysb",))
            S.emit("tensor", lambda px: px.matmul(
                ps[7][:, :CV], o1x128[:, :], ysb[:, :], start=True, stop=True),
                reads=("ysb", "ones"), writes=("ps7",))
            for c in range(ICH):
                acc = accA if c % 2 == 0 else accB
                acr = "accA" if c % 2 == 0 else "accB"
                S.emit("vector", lambda px, c=c, acc=acc, ct=ct:
                       px.scalar_tensor_tensor(
                           acc[:, :CV], ct[:, c * CV:(c + 1) * CV],
                           dpk_s[:, c:c + 1], ps[7][:, :CV], OP.mult, OP.add),
                       reads=(ctres, "dpk", "ps7"), writes=(acr,))
                S.emit("vector", lambda px, c=c, acc=acc, ct=ct, sgb=sg[p]:
                       px.tensor_tensor(
                           ct[:, c * CV:(c + 1) * CV], acc[:, :CV],
                           sgb[:, c * CV:(c + 1) * CV], OP.mult),
                       reads=(acr, f"sg{p}"), writes=(ctres,))

        def s2_back(j):
            p = j % 2
            ct = convt[p]
            ctres = f"convt{p}"
            x0 = x0t[p]
            xres = f"x0t{p}"

            def evac_o(o, pst, bres):
                S.emit("vector", lambda px, o=o, pst=pst, x0=x0: px.tensor_tensor(
                    resid[:, o * CV:(o + 1) * CV], x0[:, o * CV:(o + 1) * CV],
                    pst[:, :CV], OP.add),
                    reads=(bres, xres), writes=("resid",))
            mm_phase(lambda kc, c0, c1: w_out[kc * 128:(kc + 1) * 128, c0:c1],
                     ICH, lambda kc, ct=ct: ct[:, kc * CV:(kc + 1) * CV], ctres,
                     FCH, CV, evac_o)
            if j + 2 < NT:
                s2_load(j + 2)
            ln_block(resid, "resid", x1t, "x1t", CV)
            if j == 0:
                for f in range(FCH):
                    S.emit("vector", lambda px, f=f: px.tensor_scalar(
                        x1t[:, f * CV:f * CV + 1], x1t[:, f * CV:f * CV + 1],
                        mf_s[:, 0:1], None, OP.mult),
                        reads=("x1t", "mf"), writes=("x1t",))
            for f in range(FCH):
                S.emit("vector", lambda px, f=f: px.tensor_tensor(
                    dbuf[:, f * TT:(f + 1) * TT],
                    x1t[:, f * CV + 1:f * CV + 1 + TT],
                    x1t[:, f * CV:f * CV + TT], OP.subtract),
                    reads=("x1t",), writes=("dbuf",))
            for f in range(FCH):
                S.emit("vector", lambda px, f=f: px.scalar_tensor_tensor(
                    rbuf[:, f * TT:(f + 1) * TT], dbuf[:, f * TT:(f + 1) * TT],
                    mur_s[:, f:f + 1], x1t[:, f * CV:f * CV + TT],
                    OP.mult, OP.add),
                    reads=("dbuf", "mur", "x1t"), writes=("rbuf",))
            for f in range(FCH):
                S.emit("vector", lambda px, f=f: px.scalar_tensor_tensor(
                    dbuf[:, f * TT:(f + 1) * TT], dbuf[:, f * TT:(f + 1) * TT],
                    muk_s[:, f:f + 1], x1t[:, f * CV:f * CV + TT],
                    OP.mult, OP.add),
                    reads=("dbuf", "muk", "x1t"), writes=("dbuf",))

            def evac_k(o, pst, bres):
                S.emit("scalar", lambda px, o=o, pst=pst: px.activation(
                    xkb[:, o * TT:(o + 1) * TT], pst[:, :TT], AF.Relu),
                    reads=(bres,), writes=("xkb",))
            mm_phase(lambda kc, c0, c1: wkT[kc * 128:(kc + 1) * 128, c0:c1],
                     FCH, lambda kc: dbuf[:, kc * TT:(kc + 1) * TT], "dbuf",
                     FCH, TT, evac_k)
            for o in range(FCH):
                S.emit("scalar", lambda px, o=o: px.activation(
                    xkb[:, o * TT:(o + 1) * TT], xkb[:, o * TT:(o + 1) * TT],
                    AF.Square),
                    reads=("xkb",), writes=("xkb",))

            def evac_r(o, pst, bres):
                S.emit("scalar", lambda px, o=o, pst=pst: px.activation(
                    dbuf[:, o * TT:(o + 1) * TT], pst[:, :TT], AF.Sigmoid),
                    reads=(bres,), writes=("dbuf",))
            mm_phase(lambda kc, c0, c1: wrT[kc * 128:(kc + 1) * 128, c0:c1],
                     FCH, lambda kc: rbuf[:, kc * TT:(kc + 1) * TT], "rbuf",
                     FCH, TT, evac_r)

            def evac_v(o, pst, bres):
                S.emit("vector", lambda px, o=o, pst=pst: px.tensor_tensor(
                    rbuf[:, o * TT:(o + 1) * TT], dbuf[:, o * TT:(o + 1) * TT],
                    pst[:, :TT], OP.mult),
                    reads=(bres, "dbuf"), writes=("rbuf",))
            mm_phase(lambda kc, c0, c1: wvT[kc * 128:(kc + 1) * 128, c0:c1],
                     FCH, lambda kc: xkb[:, kc * TT:(kc + 1) * TT], "xkb",
                     FCH, TT, evac_v)
            for f in range(FCH):
                S.emit("vector", lambda px, f=f: px.tensor_tensor(
                    rbuf[:, f * TT:(f + 1) * TT],
                    x1t[:, f * CV + 1:f * CV + 1 + TT],
                    rbuf[:, f * TT:(f + 1) * TT], OP.add),
                    reads=("x1t", "rbuf"), writes=("rbuf",))
            ln_block(rbuf, "rbuf", dbuf, "dbuf", TT)
            return dma("gpsimd", outT_r[:, :, j * TT:(j + 1) * TT],
                       dbuf.rearrange("p (f t) -> p f t", f=FCH),
                       ("dbuf",), ("outs",))

        last_out = None
        s2_front_a(0)
        s2_front_a(1)
        emit_collective()
        for j in range(NT + 1):
            if j < NT:
                if j >= 2:
                    s2_front_a(j)
                s2_front_b(j)
            if 1 <= j:
                last_out = s2_back(j - 1)

        dma("gpsimd", dbg[:, 35:36], hin[:, :], ("hin",), ("dbg",), slow=True)
        S.q["gpsimd"].append({"wait": last_out})

        S.finalize()
        sems = {name: E(nc.semaphore(f"sem_{name}")) for name in S.all_sems()}
        with nc.Block() as block:
            block.sync(S.body("sync", sems))
            block.gpsimd(S.body("gpsimd", sems))
            block.tensor(S.body("tensor", sems))
            block.vector(S.body("vector", sems))
            block.scalar(S.body("scalar", sems))
    return nc


def kernel(L0_attn_input_tensor, L0_attn_W_in, L0_attn_conv_w, L0_attn_conv_b,
           L0_attn_W_x, L0_attn_A_log, L0_attn_D_param, L0_attn_W_out,
           L0_ffn_W_r, L0_ffn_W_k, L0_ffn_W_v, L0_ffn_mu_r, L0_ffn_mu_k):
    f32 = np.float32
    x0 = np.asarray(L0_attn_input_tensor, f32)[0]
    xT = np.ascontiguousarray(x0.T)
    xin_full = np.concatenate([np.zeros((DM, 4), f32), xT], axis=1)

    w_in = np.ascontiguousarray(np.asarray(L0_attn_W_in, f32))
    w_out = np.ascontiguousarray(np.asarray(L0_attn_W_out, f32))
    wrT = np.ascontiguousarray(np.asarray(L0_ffn_W_r, f32).T)
    wkT = np.ascontiguousarray(np.asarray(L0_ffn_W_k, f32).T)
    wvT = np.ascontiguousarray(np.asarray(L0_ffn_W_v, f32).T)
    wxp = np.ascontiguousarray(
        np.asarray(L0_attn_W_x, f32).reshape(ICH, 128, 33)
        .transpose(1, 0, 2).reshape(128, ICH * 33))
    dpk = np.ascontiguousarray(np.asarray(L0_attn_D_param, f32).reshape(ICH, 128).T)
    cbk = np.ascontiguousarray(np.asarray(L0_attn_conv_b, f32).reshape(ICH, 128).T)
    cwk = np.ascontiguousarray(
        np.asarray(L0_attn_conv_w, f32).reshape(ICH, 128, 4)
        .transpose(1, 0, 2).reshape(128, ICH * 4))
    murp = np.ascontiguousarray(np.asarray(L0_ffn_mu_r, f32).reshape(FCH, 128).T)
    mukp = np.ascontiguousarray(np.asarray(L0_ffn_mu_k, f32).reshape(FCH, 128).T)
    alog = np.ascontiguousarray(np.asarray(L0_attn_A_log, f32).reshape(NS, 1))

    nc = build_nc()
    in_maps = []
    for k in range(NCORES):
        sel = np.zeros((NS, NCORES), f32)
        if k > 0:
            sel[:, k - 1] = 1.0
        mfv = np.full((128, 1), 0.0 if k == 0 else 1.0, f32)
        xin_c = np.ascontiguousarray(xin_full[:, k * TC:k * TC + TC + 4])
        in_maps.append({
            "xin": xin_c, "w_in": w_in, "w_out": w_out,
            "wrT": wrT, "wkT": wkT, "wvT": wvT, "wxp": wxp,
            "dpk": dpk, "cbk": cbk, "cwk": cwk,
            "murp": murp, "mukp": mukp, "alog": alog,
            "sel8": sel, "mfirst": mfv,
        })
    res = run_bass_kernel_spmd(nc, in_maps, core_ids=list(range(NCORES)))
    out = np.concatenate([np.asarray(res.results[k]["outT"])
                          for k in range(NCORES)], axis=1)
    return np.ascontiguousarray(out.T).reshape(1, TC * NCORES, DM).astype(f32)


# version 13
# speedup vs baseline: 1.1025x; 1.1025x over previous
import numpy as np
import concourse.bass as bass
import concourse.mybir as mybir
from concourse.bass_utils import run_bass_kernel_spmd
from contextlib import ExitStack

F32 = mybir.dt.float32
AF = mybir.ActivationFunctionType
OP = mybir.AluOpType
AX = mybir.AxisListType

NCORES = 8
TC = 2048
TT = 256
NT = TC // TT
DM = 1024
FCH = DM // 128
DI = 2048
ICH = DI // 128
NS = 16
C1 = TT + 4
CV = TT + 1
EPS = 1e-5


class Sched:
    ENGS = ("sync", "gpsimd", "tensor", "vector", "scalar")

    def __init__(self):
        self.q = {e: [] for e in self.ENGS}
        self.wr = {}
        self.rd = {}
        self.seq = {"__g": 0}

    def emit(self, eng, fn, reads=(), writes=(), dma=False, async_=False,
             sem=None):
        deps = {}
        for r in reads:
            w = self.wr.get(r)
            if w is not None and (w["eng"] != eng or w["async"]):
                deps[id(w)] = w
        for t in writes:
            w = self.wr.get(t)
            if w is not None and (w["eng"] != eng or w["async"]):
                deps[id(w)] = w
            for rr in self.rd.get(t, ()):
                if rr["eng"] != eng or rr["async"]:
                    deps[id(rr)] = rr
        best = {}
        for d in deps.values():
            b = best.get(d["sem"])
            if b is None or d["gseq"] > b["gseq"]:
                best[d["sem"]] = d
        it = {"eng": eng, "fn": fn, "inc": 16 if dma else (1 if async_ else 0),
              "async": dma or async_, "sem": sem or eng,
              "gseq": self.seq["__g"]}
        self.seq["__g"] = self.seq["__g"] + 1
        for d in best.values():
            if d is it:
                continue
            if d["inc"] == 0:
                d["inc"] = 1
            self.q[eng].append({"wait": d})
        self.q[eng].append(it)
        for r in reads:
            self.rd.setdefault(r, []).append(it)
        for t in writes:
            self.wr[t] = it
            self.rd[t] = []
        return it

    def all_sems(self):
        names = set()
        for e in self.ENGS:
            for it in self.q[e]:
                if "wait" not in it:
                    names.add(it["sem"])
        return sorted(names)

    def finalize(self):
        cnt = {}
        for e in self.ENGS:
            for it in self.q[e]:
                if "wait" not in it:
                    it["ord"] = None
        allitems = []
        for e in self.ENGS:
            allitems.extend(it for it in self.q[e] if "wait" not in it)
        allitems.sort(key=lambda it: it["gseq"])
        for it in allitems:
            c = cnt.get(it["sem"], 0) + it["inc"]
            cnt[it["sem"]] = c
            it["cnt"] = c

    def body(self, eng, sems):
        items = self.q[eng]

        def run(px):
            hi = {}
            for it in items:
                if "wait" in it:
                    d = it["wait"]
                    v = d["cnt"]
                    ts = d["sem"]
                    if v > hi.get(ts, 0):
                        hi[ts] = v
                        px.wait_ge(sems[ts], v)
                else:
                    h = it["fn"](px)
                    if it["inc"]:
                        h.then_inc(sems[it["sem"]], it["inc"])
        return run


def build_nc():
    nc = bass.Bass("TRN2", target_bir_lowering=False, debug=False,
                   detect_race_conditions=False)

    xin = nc.dram_tensor("xin", [DM, TC + 4], F32, kind="ExternalInput").ap()
    w_in = nc.dram_tensor("w_in", [DM, 2 * DI], F32, kind="ExternalInput").ap()
    w_out = nc.dram_tensor("w_out", [DI, DM], F32, kind="ExternalInput").ap()
    wrT = nc.dram_tensor("wrT", [DM, DM], F32, kind="ExternalInput").ap()
    wkT = nc.dram_tensor("wkT", [DM, DM], F32, kind="ExternalInput").ap()
    wvT = nc.dram_tensor("wvT", [DM, DM], F32, kind="ExternalInput").ap()
    wxp = nc.dram_tensor("wxp", [128, ICH * 33], F32, kind="ExternalInput").ap()
    dpk = nc.dram_tensor("dpk", [128, ICH], F32, kind="ExternalInput").ap()
    cbk = nc.dram_tensor("cbk", [128, ICH], F32, kind="ExternalInput").ap()
    cwk = nc.dram_tensor("cwk", [128, ICH * 4], F32, kind="ExternalInput").ap()
    murp = nc.dram_tensor("murp", [128, FCH], F32, kind="ExternalInput").ap()
    mukp = nc.dram_tensor("mukp", [128, FCH], F32, kind="ExternalInput").ap()
    alog = nc.dram_tensor("alog", [NS, 1], F32, kind="ExternalInput").ap()
    sel8 = nc.dram_tensor("sel8", [NS, NCORES], F32, kind="ExternalInput").ap()
    mfirst = nc.dram_tensor("mfirst", [128, 1], F32, kind="ExternalInput").ap()
    outT = nc.dram_tensor("outT", [DM, TC], F32, kind="ExternalOutput").ap()

    convsp = nc.dram_tensor("convsp", [NT, 128, ICH * CV], F32)
    agin = nc.dram_tensor("agin", [1, 2 * NS], F32)
    agout = nc.dram_tensor("agout", [NCORES, 2 * NS], F32)

    xin_r = xin.rearrange("(f p) t -> p f t", p=128)
    outT_r = outT.rearrange("(f p) t -> p f t", p=128)

    S = Sched()
    with ExitStack() as ctx:
        E = ctx.enter_context
        x0t = [E(nc.sbuf_tensor(f"x0t{i}", [128, FCH * C1], F32)) for i in (0, 1)]
        convin = E(nc.sbuf_tensor("convin", [128, ICH * C1], F32))
        convt = [E(nc.sbuf_tensor(f"convt{i}", [128, ICH * CV], F32)) for i in (0, 1)]
        sg = [E(nc.sbuf_tensor(f"sg{i}", [128, ICH * CV], F32)) for i in (0, 1)]
        accA = E(nc.sbuf_tensor("accA", [128, CV], F32))
        accB = E(nc.sbuf_tensor("accB", [128, CV], F32))
        xdbl = E(nc.sbuf_tensor("xdbl", [33, CV], F32))
        bcs = E(nc.sbuf_tensor("bcs", [NS, 2 * CV], F32))
        d16 = E(nc.sbuf_tensor("d16", [NS, CV], F32))
        dAe = E(nc.sbuf_tensor("dAe", [NS, CV], F32))
        dBx = E(nc.sbuf_tensor("dBx", [NS, CV], F32))
        zer = E(nc.sbuf_tensor("zer", [NS, CV], F32))
        gb = E(nc.sbuf_tensor("gb", [NS, CV], F32))
        pb = E(nc.sbuf_tensor("pb", [NS, CV], F32))
        hb = E(nc.sbuf_tensor("hb", [NS, CV], F32))
        ycb = E(nc.sbuf_tensor("ycb", [NS, CV], F32))
        mtb = E(nc.sbuf_tensor("mtb", [NS, CV], F32))
        Mall = E(nc.sbuf_tensor("Mall", [NS, NT * CV], F32))
        ylall = E(nc.sbuf_tensor("ylall", [1, NT * CV], F32))
        Lb = E(nc.sbuf_tensor("Lb", [NS, NT + 1], F32))
        Fb = E(nc.sbuf_tensor("Fb", [NS, NT + 1], F32))
        geb = E(nc.sbuf_tensor("geb", [NS, 2], F32))
        gth = E(nc.sbuf_tensor("gth", [NS, 2 * NCORES], F32))
        hall = E(nc.sbuf_tensor("hall", [NS, NCORES], F32))
        hsel = E(nc.sbuf_tensor("hsel", [NS, NCORES], F32))
        hin = E(nc.sbuf_tensor("hin", [NS, 1], F32))
        dtm = E(nc.sbuf_tensor("dtm", [1, CV], F32))
        dsb = E(nc.sbuf_tensor("dsb", [1, CV], F32))
        expa = E(nc.sbuf_tensor("expa", [NS, 1], F32))
        nega = E(nc.sbuf_tensor("nega", [NS, 1], F32))
        resid = E(nc.sbuf_tensor("resid", [128, FCH * CV], F32))
        sqb = E(nc.sbuf_tensor("sqb", [128, FCH * CV], F32))
        x1t = E(nc.sbuf_tensor("x1t", [128, FCH * CV], F32))
        ysb = E(nc.sbuf_tensor("ysb", [1, CV], F32))
        mean = E(nc.sbuf_tensor("mean", [1, CV], F32))
        msq = E(nc.sbuf_tensor("msq", [1, CV], F32))
        var = E(nc.sbuf_tensor("var", [1, CV], F32))
        sdv = E(nc.sbuf_tensor("sdv", [1, CV], F32))
        rstd = E(nc.sbuf_tensor("rstd", [1, CV], F32))
        dbuf = E(nc.sbuf_tensor("dbuf", [128, FCH * TT], F32))
        xkb = E(nc.sbuf_tensor("xkb", [128, FCH * TT], F32))
        rbuf = E(nc.sbuf_tensor("rbuf", [128, FCH * TT], F32))
        NWB = 12
        w_ab = [E(nc.sbuf_tensor(f"w_ab{i}", [128, 384], F32)) for i in range(NWB)]
        wx_s = E(nc.sbuf_tensor("wx_s", [128, ICH * 33], F32))
        dpk_s = E(nc.sbuf_tensor("dpk_s", [128, ICH], F32))
        cbk_s = E(nc.sbuf_tensor("cbk_s", [128, ICH], F32))
        cwk_s = E(nc.sbuf_tensor("cwk_s", [128, ICH * 4], F32))
        mur_s = E(nc.sbuf_tensor("mur_s", [128, FCH], F32))
        muk_s = E(nc.sbuf_tensor("muk_s", [128, FCH], F32))
        alog_s = E(nc.sbuf_tensor("alog_s", [NS, 1], F32))
        sel_s = E(nc.sbuf_tensor("sel_s", [NS, NCORES], F32))
        mf_s = E(nc.sbuf_tensor("mf_s", [128, 1], F32))
        ones128 = E(nc.sbuf_tensor("ones128", [128, 1], F32))
        o1x128 = E(nc.sbuf_tensor("o1x128", [1, 128], F32))
        ones16 = E(nc.sbuf_tensor("ones16", [NS, 1], F32))
        o1x16 = E(nc.sbuf_tensor("o1x16", [1, NS], F32))
        ps = [E(nc.psum_tensor(f"ps{i}", [128, 512], F32)) for i in range(8)]

        def dma(eng, out, in_, reads, writes, slow=False):
            return S.emit(
                eng,
                lambda px, out=out, in_=in_, slow=slow: px.dma_start(
                    out=out, in_=in_, allow_slow_non_contiguous=slow),
                reads=reads, writes=writes, dma=True, sem="d_" + writes[0])

        wpar = [0]
        rpar = [0]

        def mm_phase(wap_fn, kcount, rhs_fn, rres, ochunks, ncols, evac_fn,
                     post_round=None):
            o0 = 0
            while o0 < ochunks:
                o1 = min(o0 + 3, ochunks)
                base = 3 * rpar[0]
                rpar[0] ^= 1
                for kc in range(kcount):
                    wb = w_ab[wpar[0]]
                    wres = f"w_ab{wpar[0]}"
                    weng = "sync" if wpar[0] % 2 == 0 else "gpsimd"
                    wpar[0] = (wpar[0] + 1) % NWB
                    nw = (o1 - o0) * 128
                    dma(weng, wb[:, 0:nw], wap_fn(kc, o0 * 128, o1 * 128),
                        (), (wres,))
                    rap = rhs_fn(kc)
                    for i in range(o1 - o0):
                        b = base + i
                        S.emit("tensor", lambda px, wb=wb, i=i, kc=kc, b=b,
                               rap=rap, nc_=ncols: px.matmul(
                                   ps[b][:, :nc_], wb[:, i * 128:(i + 1) * 128],
                                   rap, start=(kc == 0), stop=(kc == kcount - 1)),
                               reads=(wres, rres), writes=(f"ps{b}",))
                if post_round is not None:
                    post_round(o0, o1)
                for i in range(o1 - o0):
                    evac_fn(o0 + i, ps[base + i], f"ps{base + i}")
                o0 = o1

        dma("sync", wx_s[:, :], wxp[:, :], (), ("wx",))
        dma("sync", dpk_s[:, :], dpk[:, :], (), ("dpk",))
        dma("sync", cbk_s[:, :], cbk[:, :], (), ("cbk",))
        dma("sync", cwk_s[:, :], cwk[:, :], (), ("cwk",))
        dma("sync", mur_s[:, :], murp[:, :], (), ("mur",))
        dma("sync", muk_s[:, :], mukp[:, :], (), ("muk",))
        dma("sync", alog_s[:, :], alog[:, :], (), ("alog",))
        dma("sync", sel_s[:, :], sel8[:, :], (), ("sel",))
        dma("sync", mf_s[:, :], mfirst[:, :], (), ("mf",))
        for buf, val, res in ((ones128, 1.0, "ones"), (o1x128, 1.0, "ones"),
                              (ones16, 1.0, "ones"), (o1x16, 1.0, "ones"),
                              (zer, 0.0, "zer")):
            S.emit("vector", lambda px, b=buf, v=val: px.memset(b[:, :], v),
                   reads=(), writes=(res,))
        S.emit("vector", lambda px: px.memset(Lb[:, 0:1], 0.0), (), ("Lb",))
        S.emit("vector", lambda px: px.memset(Fb[:, 0:1], 1.0), (), ("Fb",))
        S.emit("scalar", lambda px: px.activation(expa[:, :], alog_s[:, :], AF.Exp),
               reads=("alog",), writes=("expa",))
        S.emit("vector", lambda px: px.tensor_scalar(
            nega[:, :], expa[:, :], -1.0, None, OP.mult),
            reads=("expa",), writes=("nega",))

        def ln_block(src, sres, dst, dres, cols):
            for f in range(FCH):
                S.emit("scalar", lambda px, f=f: px.activation(
                    sqb[:, f * CV:f * CV + cols],
                    src[:, f * cols:f * cols + cols], AF.Square),
                    reads=(sres,), writes=("sqb",))
            for f in range(FCH):
                S.emit("tensor", lambda px, f=f: px.matmul(
                    ps[6][0:1, :cols], ones128[:, :],
                    src[:, f * cols:f * cols + cols],
                    start=(f == 0), stop=(f == FCH - 1)),
                    reads=(sres, "ones"), writes=("ps6",))
            S.emit("vector", lambda px: px.tensor_scalar(
                mean[:, :cols], ps[6][0:1, :cols], 1.0 / DM, None, OP.mult),
                reads=("ps6",), writes=("mean",))
            for f in range(FCH):
                S.emit("tensor", lambda px, f=f: px.matmul(
                    ps[6][0:1, :cols], ones128[:, :],
                    sqb[:, f * CV:f * CV + cols],
                    start=(f == 0), stop=(f == FCH - 1)),
                    reads=("sqb", "ones"), writes=("ps6",))
            S.emit("vector", lambda px: px.tensor_tensor(
                msq[:, :cols], mean[:, :cols], mean[:, :cols], OP.mult),
                reads=("mean",), writes=("msq",))
            S.emit("vector", lambda px: px.tensor_scalar(
                msq[:, :cols], msq[:, :cols], EPS, None, OP.subtract),
                reads=("msq",), writes=("msq",))
            S.emit("vector", lambda px: px.scalar_tensor_tensor(
                var[:, :cols], ps[6][0:1, :cols], 1.0 / DM, msq[:, :cols],
                OP.mult, OP.subtract),
                reads=("ps6", "msq"), writes=("var",))
            S.emit("scalar", lambda px: px.activation(
                sdv[:, :cols], var[:, :cols], AF.Sqrt),
                reads=("var",), writes=("sdv",))
            S.emit("vector", lambda px: px.reciprocal(
                rstd[:, :cols], sdv[:, :cols]),
                reads=("sdv",), writes=("rstd",))
            S.emit("tensor", lambda px: px.matmul(
                ps[6][:, :cols], o1x128[:, :], mean[:, :cols],
                start=True, stop=True),
                reads=("mean", "ones"), writes=("ps6",))
            for f in range(FCH):
                S.emit("vector", lambda px, f=f: px.tensor_tensor(
                    dst[:, f * cols:f * cols + cols],
                    src[:, f * cols:f * cols + cols],
                    ps[6][:, :cols], OP.subtract),
                    reads=(sres, "ps6"), writes=(dres,))
            S.emit("tensor", lambda px: px.matmul(
                ps[6][:, :cols], o1x128[:, :], rstd[:, :cols],
                start=True, stop=True),
                reads=("rstd", "ones"), writes=("ps6",))
            for f in range(FCH):
                S.emit("vector", lambda px, f=f: px.tensor_tensor(
                    dst[:, f * cols:f * cols + cols],
                    dst[:, f * cols:f * cols + cols],
                    ps[6][:, :cols], OP.mult),
                    reads=(dres, "ps6"), writes=(dres,))

        def conv_chunk(j, c):
            p = j % 2
            ct = convt[p]
            ctres = f"convt{p}"
            bi = c * C1
            acc = accA if c % 2 == 0 else accB
            acr = "accA" if c % 2 == 0 else "accB"
            S.emit("vector", lambda px, c=c, bi=bi, acc=acc: px.tensor_scalar(
                acc[:, :CV], convin[:, bi:bi + CV],
                cwk_s[:, c * 4:c * 4 + 1], cbk_s[:, c:c + 1],
                OP.mult, OP.add),
                reads=("convin", "cwk", "cbk"), writes=(acr,))
            for k in range(1, 4):
                S.emit("vector", lambda px, c=c, bi=bi, k=k, acc=acc:
                       px.scalar_tensor_tensor(
                           acc[:, :CV], convin[:, bi + k:bi + k + CV],
                           cwk_s[:, c * 4 + k:c * 4 + k + 1], acc[:, :CV],
                           OP.mult, OP.add),
                       reads=("convin", "cwk", acr), writes=(acr,))
            S.emit("scalar", lambda px, c=c, acc=acc, ct=ct: px.activation(
                ct[:, c * CV:(c + 1) * CV], acc[:, :CV], AF.Silu),
                reads=(acr,), writes=(ctres,))
            if c == ICH - 1:
                dma("gpsimd", convsp[j, :, :], ct[:, :], (ctres,), ("spill",))

        def s1_load(j):
            p = j % 2
            dma("sync", x0t[p].rearrange("p (f t) -> p f t", f=FCH),
                xin_r[:, :, j * TT:j * TT + C1], (), (f"x0t{p}",))

        def s1_xzh1(j):
            p = j % 2
            x0 = x0t[p]
            xres = f"x0t{p}"

            def post(o0, o1):
                if j >= 1:
                    for c in range(o0, o1):
                        conv_chunk(j - 1, c)

            def evac(o, pst, bres):
                S.emit("vector", lambda px, o=o, pst=pst: px.tensor_copy(
                    convin[:, o * C1:(o + 1) * C1], pst[:, :C1]),
                    reads=(bres,), writes=("convin",))
            mm_phase(lambda kc, c0, c1: w_in[kc * 128:(kc + 1) * 128, c0:c1],
                     FCH, lambda kc: x0[:, kc * C1:(kc + 1) * C1], xres,
                     ICH, C1, evac, post_round=post)

        def s1_wx(j):
            p = j % 2
            ct = convt[p]
            ctres = f"convt{p}"
            for kc in range(ICH):
                S.emit("tensor", lambda px, kc=kc, ct=ct: px.matmul(
                    ps[6][0:33, :CV], wx_s[:, kc * 33:(kc + 1) * 33],
                    ct[:, kc * CV:(kc + 1) * CV],
                    start=(kc == 0), stop=(kc == ICH - 1)),
                    reads=("wx", ctres), writes=("ps6",))
            S.emit("vector", lambda px: px.tensor_copy(xdbl[:, :], ps[6][0:33, :CV]),
                   reads=("ps6",), writes=("xdbl",))
            dma("gpsimd", bcs[:, 0:CV], xdbl[1:17, :], ("xdbl",), ("bcs",))
            dma("gpsimd", bcs[:, CV:2 * CV], xdbl[17:33, :], ("xdbl",), ("bcs",))
            S.emit("scalar", lambda px: px.activation(dtm[:, :], xdbl[0:1, :], AF.Exp),
                   reads=("xdbl",), writes=("dtm",))
            S.emit("scalar", lambda px: px.activation(dsb[:, :], dtm[:, :], AF.Ln,
                                                      bias=1.0),
                   reads=("dtm",), writes=("dsb",))
            S.emit("tensor", lambda px: px.matmul(
                ps[7][0:NS, :CV], o1x16[:, :], dsb[:, :], start=True, stop=True),
                reads=("dsb", "ones"), writes=("ps7",))
            S.emit("vector", lambda px: px.tensor_copy(d16[:, :], ps[7][0:NS, :CV]),
                   reads=("ps7",), writes=("d16",))

        def s1_scan(j):
            p = j % 2
            ct = convt[p]
            ctres = f"convt{p}"
            S.emit("vector", lambda px: px.tensor_scalar(
                dAe[:, :], d16[:, :], nega[:, 0:1], None, OP.mult),
                reads=("d16", "nega"), writes=("dAe",))
            S.emit("scalar", lambda px: px.activation(dAe[:, :], dAe[:, :], AF.Exp),
                   reads=("dAe",), writes=("dAe",))
            S.emit("vector", lambda px, ct=ct: px.tensor_tensor(
                dBx[:, :], bcs[:, 0:CV], ct[0:NS, 0:CV], OP.mult),
                reads=("bcs", ctres), writes=("dBx",))
            S.emit("vector", lambda px: px.tensor_tensor(
                dBx[:, :], dBx[:, :], d16[:, :], OP.mult),
                reads=("dBx", "d16"), writes=("dBx",))
            S.emit("vector", lambda px: px.memset(dAe[:, 0:1], 1.0), ("dAe",), ("dAe",))
            S.emit("vector", lambda px: px.memset(dBx[:, 0:1], 0.0), ("dBx",), ("dBx",))
            S.emit("vector", lambda px: px.tensor_tensor_scan(
                gb[:, :], dAe[:, :], dBx[:, :], 0.0, OP.mult, OP.add),
                reads=("dAe", "dBx"), writes=("gb",))
            S.emit("vector", lambda px: px.tensor_tensor_scan(
                pb[:, :], dAe[:, :], zer[:, :], 1.0, OP.mult, OP.add),
                reads=("dAe", "zer"), writes=("pb",))
            S.emit("vector", lambda px, j=j: px.scalar_tensor_tensor(
                hb[:, :], pb[:, :], Lb[:, j:j + 1], gb[:, :], OP.mult, OP.add),
                reads=("pb", "Lb", "gb"), writes=("hb",))
            S.emit("vector", lambda px: px.tensor_tensor(
                ycb[:, :], hb[:, :], bcs[:, CV:2 * CV], OP.mult),
                reads=("hb", "bcs"), writes=("ycb",))
            S.emit("vector", lambda px: px.tensor_tensor(
                mtb[:, :], bcs[:, CV:2 * CV], pb[:, :], OP.mult),
                reads=("bcs", "pb"), writes=("mtb",))
            S.emit("vector", lambda px, j=j: px.tensor_scalar(
                Mall[:, j * CV:(j + 1) * CV], mtb[:, :], Fb[:, j:j + 1], None,
                OP.mult),
                reads=("mtb", "Fb"), writes=("Mall",))
            S.emit("vector", lambda px, j=j: px.scalar_tensor_tensor(
                Lb[:, j + 1:j + 2], pb[:, CV - 1:CV], Lb[:, j:j + 1],
                gb[:, CV - 1:CV], OP.mult, OP.add),
                reads=("pb", "gb", "Lb"), writes=("Lb",))
            S.emit("vector", lambda px, j=j: px.tensor_tensor(
                Fb[:, j + 1:j + 2], pb[:, CV - 1:CV], Fb[:, j:j + 1], OP.mult),
                reads=("pb", "Fb"), writes=("Fb",))

        def s1_yloc(j):
            S.emit("tensor", lambda px: px.matmul(
                ps[7][0:1, :CV], ones16[:, :], ycb[:, :], start=True, stop=True),
                reads=("ycb", "ones"), writes=("ps7",))
            S.emit("vector", lambda px, j=j: px.tensor_copy(
                ylall[:, j * CV:(j + 1) * CV], ps[7][0:1, :CV]),
                reads=("ps7",), writes=("ylall",))

        s1_load(0)
        for j in range(NT + 2):
            if j < NT:
                s1_xzh1(j)
                if j + 1 < NT:
                    s1_load(j + 1)
            elif j == NT:
                for c in range(ICH):
                    conv_chunk(NT - 1, c)
            if 2 <= j:
                s1_yloc(j - 2)
            if 1 <= j <= NT:
                s1_wx(j - 1)
                s1_scan(j - 1)

        def emit_collective():
            S.emit("vector", lambda px: px.tensor_copy(geb[:, 0:1], Lb[:, NT:NT + 1]),
                   reads=("Lb",), writes=("geb",))
            S.emit("vector", lambda px: px.tensor_copy(geb[:, 1:2], Fb[:, NT:NT + 1]),
                   reads=("Fb",), writes=("geb",))
            dma("gpsimd", agin[:, :], geb[:, :], ("geb",), ("agin",))
            S.emit("gpsimd", lambda px: px.collective_compute(
                "AllGather", OP.bypass, replica_groups=[list(range(NCORES))],
                ins=[agin.ap()], outs=[agout.ap()]),
                reads=("agin",), writes=("agout",), async_=True)
            dma("gpsimd", gth[:, :].rearrange("p (c x) -> p c x", x=2),
                agout.ap().rearrange("c (i x) -> i c x", x=2),
                ("agout",), ("gth",), slow=True)
            def vself(it):
                if it["inc"] == 0:
                    it["inc"] = 1
                S.q["vector"].append({"wait": it})
            vself(S.emit("vector",
                         lambda px: px.tensor_copy(hall[:, 0:1], gth[:, 0:1]),
                         reads=("gth",), writes=("hall",)))
            for c in range(1, NCORES):
                vself(S.emit("vector", lambda px, c=c: px.scalar_tensor_tensor(
                    hall[:, c:c + 1], gth[:, 2 * c + 1:2 * c + 2],
                    hall[:, c - 1:c], gth[:, 2 * c:2 * c + 1], OP.mult, OP.add),
                    reads=("gth", "hall"), writes=("hall",)))
            vself(S.emit("vector", lambda px: px.tensor_tensor(
                hsel[:, :], hall[:, :], sel_s[:, :], OP.mult),
                reads=("hall", "sel"), writes=("hsel",)))
            vself(S.emit("vector", lambda px: px.tensor_tensor(
                hin[:, :], hsel[:, 0:1], hsel[:, 1:2], OP.add),
                reads=("hsel",), writes=("hin",)))
            for c in range(2, NCORES):
                vself(S.emit("vector", lambda px, c=c: px.scalar_tensor_tensor(
                    hin[:, :], hsel[:, c:c + 1], 1.0, hin[:, :], OP.mult, OP.add),
                    reads=("hsel", "hin"), writes=("hin",)))

        def s2_load(j):
            p = j % 2
            base2 = j * TT + 3
            dma("gpsimd", convt[p][:, :], convsp[j, :, :], ("spill",),
                (f"convt{p}",))
            dma("sync",
                x0t[p][:, 0:FCH * CV].rearrange("p (f t) -> p f t", f=FCH),
                xin_r[:, :, base2:base2 + CV], (), (f"x0t{p}",))

        def s2_front_a(j):
            p = j % 2
            ct = convt[p]
            ctres = f"convt{p}"
            x0 = x0t[p]
            xres = f"x0t{p}"
            if j <= 1:
                s2_load(j)

            sgb = sg[p]

            def evac(o, pst, bres):
                S.emit("scalar", lambda px, o=o, pst=pst, sgb=sgb: px.activation(
                    sgb[:, o * CV:(o + 1) * CV], pst[:, :CV], AF.Silu),
                    reads=(bres,), writes=(f"sg{p}",))
            mm_phase(lambda kc, c0, c1: w_in[kc * 128:(kc + 1) * 128,
                                             DI + c0:DI + c1],
                     FCH, lambda kc, x0=x0: x0[:, kc * CV:(kc + 1) * CV], xres,
                     ICH, CV, evac)

        def s2_front_b(j):
            p = j % 2
            ct = convt[p]
            ctres = f"convt{p}"
            S.emit("tensor", lambda px, j=j: px.matmul(
                ps[7][0:1, :CV], hin[:, :], Mall[:, j * CV:(j + 1) * CV],
                start=True, stop=True),
                reads=("hin", "Mall"), writes=("ps7",))
            S.emit("vector", lambda px, j=j: px.tensor_tensor(
                ysb[:, :], ps[7][0:1, :CV], ylall[:, j * CV:(j + 1) * CV], OP.add),
                reads=("ps7", "ylall"), writes=("ysb",))
            S.emit("tensor", lambda px: px.matmul(
                ps[7][:, :CV], o1x128[:, :], ysb[:, :], start=True, stop=True),
                reads=("ysb", "ones"), writes=("ps7",))
            for c in range(ICH):
                acc = accA if c % 2 == 0 else accB
                acr = "accA" if c % 2 == 0 else "accB"
                S.emit("vector", lambda px, c=c, acc=acc, ct=ct:
                       px.scalar_tensor_tensor(
                           acc[:, :CV], ct[:, c * CV:(c + 1) * CV],
                           dpk_s[:, c:c + 1], ps[7][:, :CV], OP.mult, OP.add),
                       reads=(ctres, "dpk", "ps7"), writes=(acr,))
                S.emit("vector", lambda px, c=c, acc=acc, ct=ct, sgb=sg[p]:
                       px.tensor_tensor(
                           ct[:, c * CV:(c + 1) * CV], acc[:, :CV],
                           sgb[:, c * CV:(c + 1) * CV], OP.mult),
                       reads=(acr, f"sg{p}"), writes=(ctres,))

        def s2_back(j):
            p = j % 2
            ct = convt[p]
            ctres = f"convt{p}"
            x0 = x0t[p]
            xres = f"x0t{p}"

            def evac_o(o, pst, bres):
                S.emit("vector", lambda px, o=o, pst=pst, x0=x0: px.tensor_tensor(
                    resid[:, o * CV:(o + 1) * CV], x0[:, o * CV:(o + 1) * CV],
                    pst[:, :CV], OP.add),
                    reads=(bres, xres), writes=("resid",))
            mm_phase(lambda kc, c0, c1: w_out[kc * 128:(kc + 1) * 128, c0:c1],
                     ICH, lambda kc, ct=ct: ct[:, kc * CV:(kc + 1) * CV], ctres,
                     FCH, CV, evac_o)
            if j + 2 < NT:
                s2_load(j + 2)
            ln_block(resid, "resid", x1t, "x1t", CV)
            if j == 0:
                for f in range(FCH):
                    S.emit("vector", lambda px, f=f: px.tensor_scalar(
                        x1t[:, f * CV:f * CV + 1], x1t[:, f * CV:f * CV + 1],
                        mf_s[:, 0:1], None, OP.mult),
                        reads=("x1t", "mf"), writes=("x1t",))
            for f in range(FCH):
                S.emit("vector", lambda px, f=f: px.tensor_tensor(
                    dbuf[:, f * TT:(f + 1) * TT],
                    x1t[:, f * CV + 1:f * CV + 1 + TT],
                    x1t[:, f * CV:f * CV + TT], OP.subtract),
                    reads=("x1t",), writes=("dbuf",))
            for f in range(FCH):
                S.emit("vector", lambda px, f=f: px.scalar_tensor_tensor(
                    rbuf[:, f * TT:(f + 1) * TT], dbuf[:, f * TT:(f + 1) * TT],
                    mur_s[:, f:f + 1], x1t[:, f * CV:f * CV + TT],
                    OP.mult, OP.add),
                    reads=("dbuf", "mur", "x1t"), writes=("rbuf",))
            for f in range(FCH):
                S.emit("vector", lambda px, f=f: px.scalar_tensor_tensor(
                    dbuf[:, f * TT:(f + 1) * TT], dbuf[:, f * TT:(f + 1) * TT],
                    muk_s[:, f:f + 1], x1t[:, f * CV:f * CV + TT],
                    OP.mult, OP.add),
                    reads=("dbuf", "muk", "x1t"), writes=("dbuf",))

            def evac_k(o, pst, bres):
                S.emit("scalar", lambda px, o=o, pst=pst: px.activation(
                    xkb[:, o * TT:(o + 1) * TT], pst[:, :TT], AF.Relu),
                    reads=(bres,), writes=("xkb",))
            mm_phase(lambda kc, c0, c1: wkT[kc * 128:(kc + 1) * 128, c0:c1],
                     FCH, lambda kc: dbuf[:, kc * TT:(kc + 1) * TT], "dbuf",
                     FCH, TT, evac_k)
            for o in range(FCH):
                S.emit("scalar", lambda px, o=o: px.activation(
                    xkb[:, o * TT:(o + 1) * TT], xkb[:, o * TT:(o + 1) * TT],
                    AF.Square),
                    reads=("xkb",), writes=("xkb",))

            def evac_r(o, pst, bres):
                S.emit("scalar", lambda px, o=o, pst=pst: px.activation(
                    dbuf[:, o * TT:(o + 1) * TT], pst[:, :TT], AF.Sigmoid),
                    reads=(bres,), writes=("dbuf",))
            mm_phase(lambda kc, c0, c1: wrT[kc * 128:(kc + 1) * 128, c0:c1],
                     FCH, lambda kc: rbuf[:, kc * TT:(kc + 1) * TT], "rbuf",
                     FCH, TT, evac_r)

            def evac_v(o, pst, bres):
                S.emit("vector", lambda px, o=o, pst=pst: px.tensor_tensor(
                    rbuf[:, o * TT:(o + 1) * TT], dbuf[:, o * TT:(o + 1) * TT],
                    pst[:, :TT], OP.mult),
                    reads=(bres, "dbuf"), writes=("rbuf",))
            mm_phase(lambda kc, c0, c1: wvT[kc * 128:(kc + 1) * 128, c0:c1],
                     FCH, lambda kc: xkb[:, kc * TT:(kc + 1) * TT], "xkb",
                     FCH, TT, evac_v)
            for f in range(FCH):
                S.emit("vector", lambda px, f=f: px.tensor_tensor(
                    rbuf[:, f * TT:(f + 1) * TT],
                    x1t[:, f * CV + 1:f * CV + 1 + TT],
                    rbuf[:, f * TT:(f + 1) * TT], OP.add),
                    reads=("x1t", "rbuf"), writes=("rbuf",))
            ln_block(rbuf, "rbuf", dbuf, "dbuf", TT)
            return dma("gpsimd", outT_r[:, :, j * TT:(j + 1) * TT],
                       dbuf.rearrange("p (f t) -> p f t", f=FCH),
                       ("dbuf",), ("outs",))

        last_out = None
        s2_front_a(0)
        s2_front_a(1)
        emit_collective()
        for j in range(NT + 1):
            if j < NT:
                if j >= 2:
                    s2_front_a(j)
                s2_front_b(j)
            if 1 <= j:
                last_out = s2_back(j - 1)

        S.q["gpsimd"].append({"wait": last_out})

        S.finalize()
        sems = {name: E(nc.semaphore(f"sem_{name}")) for name in S.all_sems()}
        with nc.Block() as block:
            block.sync(S.body("sync", sems))
            block.gpsimd(S.body("gpsimd", sems))
            block.tensor(S.body("tensor", sems))
            block.vector(S.body("vector", sems))
            block.scalar(S.body("scalar", sems))
    return nc


def kernel(L0_attn_input_tensor, L0_attn_W_in, L0_attn_conv_w, L0_attn_conv_b,
           L0_attn_W_x, L0_attn_A_log, L0_attn_D_param, L0_attn_W_out,
           L0_ffn_W_r, L0_ffn_W_k, L0_ffn_W_v, L0_ffn_mu_r, L0_ffn_mu_k):
    f32 = np.float32
    x0 = np.asarray(L0_attn_input_tensor, f32)[0]
    xT = np.ascontiguousarray(x0.T)
    xin_full = np.concatenate([np.zeros((DM, 4), f32), xT], axis=1)

    w_in = np.ascontiguousarray(np.asarray(L0_attn_W_in, f32))
    w_out = np.ascontiguousarray(np.asarray(L0_attn_W_out, f32))
    wrT = np.ascontiguousarray(np.asarray(L0_ffn_W_r, f32).T)
    wkT = np.ascontiguousarray(np.asarray(L0_ffn_W_k, f32).T)
    wvT = np.ascontiguousarray(np.asarray(L0_ffn_W_v, f32).T)
    wxp = np.ascontiguousarray(
        np.asarray(L0_attn_W_x, f32).reshape(ICH, 128, 33)
        .transpose(1, 0, 2).reshape(128, ICH * 33))
    dpk = np.ascontiguousarray(np.asarray(L0_attn_D_param, f32).reshape(ICH, 128).T)
    cbk = np.ascontiguousarray(np.asarray(L0_attn_conv_b, f32).reshape(ICH, 128).T)
    cwk = np.ascontiguousarray(
        np.asarray(L0_attn_conv_w, f32).reshape(ICH, 128, 4)
        .transpose(1, 0, 2).reshape(128, ICH * 4))
    murp = np.ascontiguousarray(np.asarray(L0_ffn_mu_r, f32).reshape(FCH, 128).T)
    mukp = np.ascontiguousarray(np.asarray(L0_ffn_mu_k, f32).reshape(FCH, 128).T)
    alog = np.ascontiguousarray(np.asarray(L0_attn_A_log, f32).reshape(NS, 1))

    nc = build_nc()
    in_maps = []
    for k in range(NCORES):
        sel = np.zeros((NS, NCORES), f32)
        if k > 0:
            sel[:, k - 1] = 1.0
        mfv = np.full((128, 1), 0.0 if k == 0 else 1.0, f32)
        xin_c = np.ascontiguousarray(xin_full[:, k * TC:k * TC + TC + 4])
        in_maps.append({
            "xin": xin_c, "w_in": w_in, "w_out": w_out,
            "wrT": wrT, "wkT": wkT, "wvT": wvT, "wxp": wxp,
            "dpk": dpk, "cbk": cbk, "cwk": cwk,
            "murp": murp, "mukp": mukp, "alog": alog,
            "sel8": sel, "mfirst": mfv,
        })
    res = run_bass_kernel_spmd(nc, in_maps, core_ids=list(range(NCORES)))
    out = np.concatenate([np.asarray(res.results[k]["outT"])
                          for k in range(NCORES)], axis=1)
    return np.ascontiguousarray(out.T).reshape(1, TC * NCORES, DM).astype(f32)


# version 14
# speedup vs baseline: 1.1127x; 1.0093x over previous
import numpy as np
import concourse.bass as bass
import concourse.mybir as mybir
from concourse.bass_utils import run_bass_kernel_spmd
from contextlib import ExitStack

F32 = mybir.dt.float32
AF = mybir.ActivationFunctionType
OP = mybir.AluOpType
AX = mybir.AxisListType

NCORES = 8
TC = 2048
TT = 256
NT = TC // TT
DM = 1024
FCH = DM // 128
DI = 2048
ICH = DI // 128
NS = 16
C1 = TT + 4
CV = TT + 1
EPS = 1e-5


class Sched:
    ENGS = ("sync", "gpsimd", "tensor", "vector", "scalar")

    def __init__(self):
        self.q = {e: [] for e in self.ENGS}
        self.wr = {}
        self.rd = {}
        self.seq = {"__g": 0}

    def emit(self, eng, fn, reads=(), writes=(), dma=False, async_=False,
             sem=None):
        deps = {}
        for r in reads:
            w = self.wr.get(r)
            if w is not None and (w["eng"] != eng or w["async"]):
                deps[id(w)] = w
        for t in writes:
            w = self.wr.get(t)
            if w is not None and (w["eng"] != eng or w["async"]):
                deps[id(w)] = w
            for rr in self.rd.get(t, ()):
                if rr["eng"] != eng or rr["async"]:
                    deps[id(rr)] = rr
        best = {}
        for d in deps.values():
            b = best.get(d["sem"])
            if b is None or d["gseq"] > b["gseq"]:
                best[d["sem"]] = d
        it = {"eng": eng, "fn": fn, "inc": 16 if dma else (1 if async_ else 0),
              "async": dma or async_, "sem": sem or eng,
              "gseq": self.seq["__g"]}
        self.seq["__g"] = self.seq["__g"] + 1
        for d in best.values():
            if d is it:
                continue
            if d["inc"] == 0:
                d["inc"] = 1
            self.q[eng].append({"wait": d})
        self.q[eng].append(it)
        for r in reads:
            self.rd.setdefault(r, []).append(it)
        for t in writes:
            self.wr[t] = it
            self.rd[t] = []
        return it

    def all_sems(self):
        names = set()
        for e in self.ENGS:
            for it in self.q[e]:
                if "wait" not in it:
                    names.add(it["sem"])
        return sorted(names)

    def finalize(self):
        cnt = {}
        for e in self.ENGS:
            for it in self.q[e]:
                if "wait" not in it:
                    it["ord"] = None
        allitems = []
        for e in self.ENGS:
            allitems.extend(it for it in self.q[e] if "wait" not in it)
        allitems.sort(key=lambda it: it["gseq"])
        for it in allitems:
            c = cnt.get(it["sem"], 0) + it["inc"]
            cnt[it["sem"]] = c
            it["cnt"] = c

    def body(self, eng, sems):
        items = self.q[eng]

        def run(px):
            hi = {}
            for it in items:
                if "wait" in it:
                    d = it["wait"]
                    v = d["cnt"]
                    ts = d["sem"]
                    if v > hi.get(ts, 0):
                        hi[ts] = v
                        px.wait_ge(sems[ts], v)
                else:
                    h = it["fn"](px)
                    if it["inc"]:
                        h.then_inc(sems[it["sem"]], it["inc"])
        return run


def build_nc():
    nc = bass.Bass("TRN2", target_bir_lowering=False, debug=False,
                   detect_race_conditions=False)

    xin = nc.dram_tensor("xin", [DM, TC + 4], F32, kind="ExternalInput").ap()
    w_in = nc.dram_tensor("w_in", [DM, 2 * DI], F32, kind="ExternalInput").ap()
    w_out = nc.dram_tensor("w_out", [DI, DM], F32, kind="ExternalInput").ap()
    wrT = nc.dram_tensor("wrT", [DM, DM], F32, kind="ExternalInput").ap()
    wkT = nc.dram_tensor("wkT", [DM, DM], F32, kind="ExternalInput").ap()
    wvT = nc.dram_tensor("wvT", [DM, DM], F32, kind="ExternalInput").ap()
    wxp = nc.dram_tensor("wxp", [128, ICH * 33], F32, kind="ExternalInput").ap()
    dpk = nc.dram_tensor("dpk", [128, ICH], F32, kind="ExternalInput").ap()
    cbk = nc.dram_tensor("cbk", [128, ICH], F32, kind="ExternalInput").ap()
    cwk = nc.dram_tensor("cwk", [128, ICH * 4], F32, kind="ExternalInput").ap()
    murp = nc.dram_tensor("murp", [128, FCH], F32, kind="ExternalInput").ap()
    mukp = nc.dram_tensor("mukp", [128, FCH], F32, kind="ExternalInput").ap()
    alog = nc.dram_tensor("alog", [NS, 1], F32, kind="ExternalInput").ap()
    sel8 = nc.dram_tensor("sel8", [NS, NCORES], F32, kind="ExternalInput").ap()
    mfirst = nc.dram_tensor("mfirst", [128, 1], F32, kind="ExternalInput").ap()
    outT = nc.dram_tensor("outT", [DM, TC], F32, kind="ExternalOutput").ap()

    convsp = nc.dram_tensor("convsp", [NT, 128, ICH * CV], F32)
    agin = nc.dram_tensor("agin", [1, 2 * NS], F32)
    agout = nc.dram_tensor("agout", [NCORES, 2 * NS], F32)

    xin_r = xin.rearrange("(f p) t -> p f t", p=128)
    outT_r = outT.rearrange("(f p) t -> p f t", p=128)

    S = Sched()
    with ExitStack() as ctx:
        E = ctx.enter_context
        x0t = [E(nc.sbuf_tensor(f"x0t{i}", [128, FCH * C1], F32)) for i in (0, 1)]
        convin = E(nc.sbuf_tensor("convin", [128, ICH * C1], F32))
        convt = [E(nc.sbuf_tensor(f"convt{i}", [128, ICH * CV], F32)) for i in (0, 1)]
        sg = [E(nc.sbuf_tensor(f"sg{i}", [128, ICH * CV], F32)) for i in (0, 1)]
        accA = E(nc.sbuf_tensor("accA", [128, CV], F32))
        accB = E(nc.sbuf_tensor("accB", [128, CV], F32))
        xdbl = E(nc.sbuf_tensor("xdbl", [33, CV], F32))
        bcs = E(nc.sbuf_tensor("bcs", [NS, 2 * CV], F32))
        d16 = E(nc.sbuf_tensor("d16", [NS, CV], F32))
        dAe = E(nc.sbuf_tensor("dAe", [NS, CV], F32))
        dBx = E(nc.sbuf_tensor("dBx", [NS, CV], F32))
        zer = E(nc.sbuf_tensor("zer", [NS, CV], F32))
        gb = E(nc.sbuf_tensor("gb", [NS, CV], F32))
        pb = E(nc.sbuf_tensor("pb", [NS, CV], F32))
        hb = E(nc.sbuf_tensor("hb", [NS, CV], F32))
        ycb = E(nc.sbuf_tensor("ycb", [NS, CV], F32))
        mtb = E(nc.sbuf_tensor("mtb", [NS, CV], F32))
        Mall = E(nc.sbuf_tensor("Mall", [NS, NT * CV], F32))
        ylall = E(nc.sbuf_tensor("ylall", [1, NT * CV], F32))
        Lb = E(nc.sbuf_tensor("Lb", [NS, NT + 1], F32))
        Fb = E(nc.sbuf_tensor("Fb", [NS, NT + 1], F32))
        geb = E(nc.sbuf_tensor("geb", [NS, 2], F32))
        gth = E(nc.sbuf_tensor("gth", [NS, 2 * NCORES], F32))
        hall = E(nc.sbuf_tensor("hall", [NS, NCORES], F32))
        hsel = E(nc.sbuf_tensor("hsel", [NS, NCORES], F32))
        hin = E(nc.sbuf_tensor("hin", [NS, 1], F32))
        dtm = E(nc.sbuf_tensor("dtm", [1, CV], F32))
        dsb = E(nc.sbuf_tensor("dsb", [1, CV], F32))
        expa = E(nc.sbuf_tensor("expa", [NS, 1], F32))
        nega = E(nc.sbuf_tensor("nega", [NS, 1], F32))
        resid = E(nc.sbuf_tensor("resid", [128, FCH * CV], F32))
        sqb = E(nc.sbuf_tensor("sqb", [128, FCH * CV], F32))
        x1t = E(nc.sbuf_tensor("x1t", [128, FCH * CV], F32))
        ysb = E(nc.sbuf_tensor("ysb", [1, CV], F32))
        mean = E(nc.sbuf_tensor("mean", [1, CV], F32))
        msq = E(nc.sbuf_tensor("msq", [1, CV], F32))
        var = E(nc.sbuf_tensor("var", [1, CV], F32))
        sdv = E(nc.sbuf_tensor("sdv", [1, CV], F32))
        rstd = E(nc.sbuf_tensor("rstd", [1, CV], F32))
        dbuf = E(nc.sbuf_tensor("dbuf", [128, FCH * TT], F32))
        xkb = E(nc.sbuf_tensor("xkb", [128, FCH * TT], F32))
        rbuf = E(nc.sbuf_tensor("rbuf", [128, FCH * TT], F32))
        NWB = 12
        w_ab = [E(nc.sbuf_tensor(f"w_ab{i}", [128, 384], F32)) for i in range(NWB)]
        wx_s = E(nc.sbuf_tensor("wx_s", [128, ICH * 33], F32))
        dpk_s = E(nc.sbuf_tensor("dpk_s", [128, ICH], F32))
        cbk_s = E(nc.sbuf_tensor("cbk_s", [128, ICH], F32))
        cwk_s = E(nc.sbuf_tensor("cwk_s", [128, ICH * 4], F32))
        mur_s = E(nc.sbuf_tensor("mur_s", [128, FCH], F32))
        muk_s = E(nc.sbuf_tensor("muk_s", [128, FCH], F32))
        alog_s = E(nc.sbuf_tensor("alog_s", [NS, 1], F32))
        sel_s = E(nc.sbuf_tensor("sel_s", [NS, NCORES], F32))
        mf_s = E(nc.sbuf_tensor("mf_s", [128, 1], F32))
        ones128 = E(nc.sbuf_tensor("ones128", [128, 1], F32))
        o1x128 = E(nc.sbuf_tensor("o1x128", [1, 128], F32))
        ones16 = E(nc.sbuf_tensor("ones16", [NS, 1], F32))
        o1x16 = E(nc.sbuf_tensor("o1x16", [1, NS], F32))
        ps = [E(nc.psum_tensor(f"ps{i}", [128, 512], F32)) for i in range(8)]

        def dma(eng, out, in_, reads, writes, slow=False):
            return S.emit(
                eng,
                lambda px, out=out, in_=in_, slow=slow: px.dma_start(
                    out=out, in_=in_, allow_slow_non_contiguous=slow),
                reads=reads, writes=writes, dma=True, sem="d_" + writes[0])

        wpar = [0]
        rpar = [0]

        def mm_phase(wap_fn, kcount, rhs_fn, rres, ochunks, ncols, evac_fn,
                     post_round=None):
            o0 = 0
            while o0 < ochunks:
                o1 = min(o0 + 3, ochunks)
                base = 3 * rpar[0]
                rpar[0] ^= 1
                for kc in range(kcount):
                    wb = w_ab[wpar[0]]
                    wres = f"w_ab{wpar[0]}"
                    weng = "sync" if wpar[0] % 2 == 0 else "gpsimd"
                    wpar[0] = (wpar[0] + 1) % NWB
                    nw = (o1 - o0) * 128
                    dma(weng, wb[:, 0:nw], wap_fn(kc, o0 * 128, o1 * 128),
                        (), (wres,))
                    rap = rhs_fn(kc)
                    for i in range(o1 - o0):
                        b = base + i
                        S.emit("tensor", lambda px, wb=wb, i=i, kc=kc, b=b,
                               rap=rap, nc_=ncols: px.matmul(
                                   ps[b][:, :nc_], wb[:, i * 128:(i + 1) * 128],
                                   rap, start=(kc == 0), stop=(kc == kcount - 1)),
                               reads=(wres, rres), writes=(f"ps{b}",))
                if post_round is not None:
                    post_round(o0, o1)
                for i in range(o1 - o0):
                    evac_fn(o0 + i, ps[base + i], f"ps{base + i}")
                o0 = o1

        dma("sync", wx_s[:, :], wxp[:, :], (), ("wx",))
        dma("sync", dpk_s[:, :], dpk[:, :], (), ("dpk",))
        dma("sync", cbk_s[:, :], cbk[:, :], (), ("cbk",))
        dma("sync", cwk_s[:, :], cwk[:, :], (), ("cwk",))
        dma("sync", mur_s[:, :], murp[:, :], (), ("mur",))
        dma("sync", muk_s[:, :], mukp[:, :], (), ("muk",))
        dma("sync", alog_s[:, :], alog[:, :], (), ("alog",))
        dma("sync", sel_s[:, :], sel8[:, :], (), ("sel",))
        dma("sync", mf_s[:, :], mfirst[:, :], (), ("mf",))
        for buf, val, res in ((ones128, 1.0, "ones"), (o1x128, 1.0, "ones"),
                              (ones16, 1.0, "ones"), (o1x16, 1.0, "ones"),
                              (zer, 0.0, "zer")):
            S.emit("vector", lambda px, b=buf, v=val: px.memset(b[:, :], v),
                   reads=(), writes=(res,))
        S.emit("vector", lambda px: px.memset(Lb[:, 0:1], 0.0), (), ("Lb",))
        S.emit("vector", lambda px: px.memset(Fb[:, 0:1], 1.0), (), ("Fb",))
        S.emit("scalar", lambda px: px.activation(expa[:, :], alog_s[:, :], AF.Exp),
               reads=("alog",), writes=("expa",))
        S.emit("vector", lambda px: px.tensor_scalar(
            nega[:, :], expa[:, :], -1.0, None, OP.mult),
            reads=("expa",), writes=("nega",))

        def ln_block(src, sres, dst, dres, cols):
            for f in range(FCH):
                S.emit("scalar", lambda px, f=f: px.activation(
                    sqb[:, f * CV:f * CV + cols],
                    src[:, f * cols:f * cols + cols], AF.Square),
                    reads=(sres,), writes=("sqb",))
            for f in range(FCH):
                S.emit("tensor", lambda px, f=f: px.matmul(
                    ps[6][0:1, :cols], ones128[:, :],
                    src[:, f * cols:f * cols + cols],
                    start=(f == 0), stop=(f == FCH - 1)),
                    reads=(sres, "ones"), writes=("ps6",))
            S.emit("vector", lambda px: px.tensor_scalar(
                mean[:, :cols], ps[6][0:1, :cols], 1.0 / DM, None, OP.mult),
                reads=("ps6",), writes=("mean",))
            for f in range(FCH):
                S.emit("tensor", lambda px, f=f: px.matmul(
                    ps[6][0:1, :cols], ones128[:, :],
                    sqb[:, f * CV:f * CV + cols],
                    start=(f == 0), stop=(f == FCH - 1)),
                    reads=("sqb", "ones"), writes=("ps6",))
            S.emit("vector", lambda px: px.tensor_tensor(
                msq[:, :cols], mean[:, :cols], mean[:, :cols], OP.mult),
                reads=("mean",), writes=("msq",))
            S.emit("vector", lambda px: px.tensor_scalar(
                msq[:, :cols], msq[:, :cols], EPS, None, OP.subtract),
                reads=("msq",), writes=("msq",))
            S.emit("vector", lambda px: px.scalar_tensor_tensor(
                var[:, :cols], ps[6][0:1, :cols], 1.0 / DM, msq[:, :cols],
                OP.mult, OP.subtract),
                reads=("ps6", "msq"), writes=("var",))
            S.emit("scalar", lambda px: px.activation(
                sdv[:, :cols], var[:, :cols], AF.Sqrt),
                reads=("var",), writes=("sdv",))
            S.emit("vector", lambda px: px.reciprocal(
                rstd[:, :cols], sdv[:, :cols]),
                reads=("sdv",), writes=("rstd",))
            S.emit("tensor", lambda px: px.matmul(
                ps[6][:, :cols], o1x128[:, :], mean[:, :cols],
                start=True, stop=True),
                reads=("mean", "ones"), writes=("ps6",))
            for f in range(FCH):
                S.emit("vector", lambda px, f=f: px.tensor_tensor(
                    dst[:, f * cols:f * cols + cols],
                    src[:, f * cols:f * cols + cols],
                    ps[6][:, :cols], OP.subtract),
                    reads=(sres, "ps6"), writes=(dres,))
            S.emit("tensor", lambda px: px.matmul(
                ps[7][:, :cols], o1x128[:, :], rstd[:, :cols],
                start=True, stop=True),
                reads=("rstd", "ones"), writes=("ps7",))
            for f in range(FCH):
                S.emit("vector", lambda px, f=f: px.tensor_tensor(
                    dst[:, f * cols:f * cols + cols],
                    dst[:, f * cols:f * cols + cols],
                    ps[7][:, :cols], OP.mult),
                    reads=(dres, "ps7"), writes=(dres,))

        def conv_chunk(j, c):
            p = j % 2
            ct = convt[p]
            ctres = f"convt{p}"
            bi = c * C1
            acc = accA if c % 2 == 0 else accB
            acr = "accA" if c % 2 == 0 else "accB"
            S.emit("vector", lambda px, c=c, bi=bi, acc=acc: px.tensor_scalar(
                acc[:, :CV], convin[:, bi:bi + CV],
                cwk_s[:, c * 4:c * 4 + 1], cbk_s[:, c:c + 1],
                OP.mult, OP.add),
                reads=("convin", "cwk", "cbk"), writes=(acr,))
            for k in range(1, 4):
                S.emit("vector", lambda px, c=c, bi=bi, k=k, acc=acc:
                       px.scalar_tensor_tensor(
                           acc[:, :CV], convin[:, bi + k:bi + k + CV],
                           cwk_s[:, c * 4 + k:c * 4 + k + 1], acc[:, :CV],
                           OP.mult, OP.add),
                       reads=("convin", "cwk", acr), writes=(acr,))
            S.emit("scalar", lambda px, c=c, acc=acc, ct=ct: px.activation(
                ct[:, c * CV:(c + 1) * CV], acc[:, :CV], AF.Silu),
                reads=(acr,), writes=(ctres,))
            if c == ICH - 1:
                dma("gpsimd", convsp[j, :, :], ct[:, :], (ctres,), ("spill",))

        def s1_load(j):
            p = j % 2
            dma("sync", x0t[p].rearrange("p (f t) -> p f t", f=FCH),
                xin_r[:, :, j * TT:j * TT + C1], (), (f"x0t{p}",))

        def s1_xzh1(j):
            p = j % 2
            x0 = x0t[p]
            xres = f"x0t{p}"

            def post(o0, o1):
                if j >= 1:
                    for c in range(o0, o1):
                        conv_chunk(j - 1, c)

            def evac(o, pst, bres):
                S.emit("vector", lambda px, o=o, pst=pst: px.tensor_copy(
                    convin[:, o * C1:(o + 1) * C1], pst[:, :C1]),
                    reads=(bres,), writes=("convin",))
            mm_phase(lambda kc, c0, c1: w_in[kc * 128:(kc + 1) * 128, c0:c1],
                     FCH, lambda kc: x0[:, kc * C1:(kc + 1) * C1], xres,
                     ICH, C1, evac, post_round=post)

        def s1_wx(j):
            p = j % 2
            ct = convt[p]
            ctres = f"convt{p}"
            for kc in range(ICH):
                S.emit("tensor", lambda px, kc=kc, ct=ct: px.matmul(
                    ps[6][0:33, :CV], wx_s[:, kc * 33:(kc + 1) * 33],
                    ct[:, kc * CV:(kc + 1) * CV],
                    start=(kc == 0), stop=(kc == ICH - 1)),
                    reads=("wx", ctres), writes=("ps6",))
            S.emit("vector", lambda px: px.tensor_copy(xdbl[:, :], ps[6][0:33, :CV]),
                   reads=("ps6",), writes=("xdbl",))
            dma("gpsimd", bcs[:, 0:CV], xdbl[1:17, :], ("xdbl",), ("bcs",))
            dma("gpsimd", bcs[:, CV:2 * CV], xdbl[17:33, :], ("xdbl",), ("bcs",))
            S.emit("scalar", lambda px: px.activation(dtm[:, :], xdbl[0:1, :], AF.Exp),
                   reads=("xdbl",), writes=("dtm",))
            S.emit("scalar", lambda px: px.activation(dsb[:, :], dtm[:, :], AF.Ln,
                                                      bias=1.0),
                   reads=("dtm",), writes=("dsb",))
            S.emit("tensor", lambda px: px.matmul(
                ps[7][0:NS, :CV], o1x16[:, :], dsb[:, :], start=True, stop=True),
                reads=("dsb", "ones"), writes=("ps7",))
            S.emit("vector", lambda px: px.tensor_copy(d16[:, :], ps[7][0:NS, :CV]),
                   reads=("ps7",), writes=("d16",))

        def s1_scan(j):
            p = j % 2
            ct = convt[p]
            ctres = f"convt{p}"
            S.emit("vector", lambda px: px.tensor_scalar(
                dAe[:, :], d16[:, :], nega[:, 0:1], None, OP.mult),
                reads=("d16", "nega"), writes=("dAe",))
            S.emit("scalar", lambda px: px.activation(dAe[:, :], dAe[:, :], AF.Exp),
                   reads=("dAe",), writes=("dAe",))
            S.emit("vector", lambda px, ct=ct: px.tensor_tensor(
                dBx[:, :], bcs[:, 0:CV], ct[0:NS, 0:CV], OP.mult),
                reads=("bcs", ctres), writes=("dBx",))
            S.emit("vector", lambda px: px.tensor_tensor(
                dBx[:, :], dBx[:, :], d16[:, :], OP.mult),
                reads=("dBx", "d16"), writes=("dBx",))
            S.emit("vector", lambda px: px.memset(dAe[:, 0:1], 1.0), ("dAe",), ("dAe",))
            S.emit("vector", lambda px: px.memset(dBx[:, 0:1], 0.0), ("dBx",), ("dBx",))
            S.emit("vector", lambda px: px.tensor_tensor_scan(
                gb[:, :], dAe[:, :], dBx[:, :], 0.0, OP.mult, OP.add),
                reads=("dAe", "dBx"), writes=("gb",))
            S.emit("vector", lambda px: px.tensor_tensor_scan(
                pb[:, :], dAe[:, :], zer[:, :], 1.0, OP.mult, OP.add),
                reads=("dAe", "zer"), writes=("pb",))
            S.emit("vector", lambda px, j=j: px.scalar_tensor_tensor(
                hb[:, :], pb[:, :], Lb[:, j:j + 1], gb[:, :], OP.mult, OP.add),
                reads=("pb", "Lb", "gb"), writes=("hb",))
            S.emit("vector", lambda px: px.tensor_tensor(
                ycb[:, :], hb[:, :], bcs[:, CV:2 * CV], OP.mult),
                reads=("hb", "bcs"), writes=("ycb",))
            S.emit("vector", lambda px: px.tensor_tensor(
                mtb[:, :], bcs[:, CV:2 * CV], pb[:, :], OP.mult),
                reads=("bcs", "pb"), writes=("mtb",))
            S.emit("vector", lambda px, j=j: px.tensor_scalar(
                Mall[:, j * CV:(j + 1) * CV], mtb[:, :], Fb[:, j:j + 1], None,
                OP.mult),
                reads=("mtb", "Fb"), writes=("Mall",))
            S.emit("vector", lambda px, j=j: px.scalar_tensor_tensor(
                Lb[:, j + 1:j + 2], pb[:, CV - 1:CV], Lb[:, j:j + 1],
                gb[:, CV - 1:CV], OP.mult, OP.add),
                reads=("pb", "gb", "Lb"), writes=("Lb",))
            S.emit("vector", lambda px, j=j: px.tensor_tensor(
                Fb[:, j + 1:j + 2], pb[:, CV - 1:CV], Fb[:, j:j + 1], OP.mult),
                reads=("pb", "Fb"), writes=("Fb",))

        def s1_yloc(j):
            S.emit("tensor", lambda px: px.matmul(
                ps[7][0:1, :CV], ones16[:, :], ycb[:, :], start=True, stop=True),
                reads=("ycb", "ones"), writes=("ps7",))
            S.emit("vector", lambda px, j=j: px.tensor_copy(
                ylall[:, j * CV:(j + 1) * CV], ps[7][0:1, :CV]),
                reads=("ps7",), writes=("ylall",))

        s1_load(0)
        for j in range(NT + 2):
            if j < NT:
                s1_xzh1(j)
                if j + 1 < NT:
                    s1_load(j + 1)
            elif j == NT:
                for c in range(ICH):
                    conv_chunk(NT - 1, c)
            if 2 <= j:
                s1_yloc(j - 2)
            if 1 <= j <= NT:
                s1_wx(j - 1)
                s1_scan(j - 1)

        def emit_collective():
            S.emit("vector", lambda px: px.tensor_copy(geb[:, 0:1], Lb[:, NT:NT + 1]),
                   reads=("Lb",), writes=("geb",))
            S.emit("vector", lambda px: px.tensor_copy(geb[:, 1:2], Fb[:, NT:NT + 1]),
                   reads=("Fb",), writes=("geb",))
            dma("gpsimd", agin[:, :], geb[:, :], ("geb",), ("agin",))
            S.emit("gpsimd", lambda px: px.collective_compute(
                "AllGather", OP.bypass, replica_groups=[list(range(NCORES))],
                ins=[agin.ap()], outs=[agout.ap()]),
                reads=("agin",), writes=("agout",), async_=True)
            dma("gpsimd", gth[:, :].rearrange("p (c x) -> p c x", x=2),
                agout.ap().rearrange("c (i x) -> i c x", x=2),
                ("agout",), ("gth",), slow=True)
            def vself(it):
                if it["inc"] == 0:
                    it["inc"] = 1
                S.q["vector"].append({"wait": it})
            vself(S.emit("vector",
                         lambda px: px.tensor_copy(hall[:, 0:1], gth[:, 0:1]),
                         reads=("gth",), writes=("hall",)))
            for c in range(1, NCORES):
                vself(S.emit("vector", lambda px, c=c: px.scalar_tensor_tensor(
                    hall[:, c:c + 1], gth[:, 2 * c + 1:2 * c + 2],
                    hall[:, c - 1:c], gth[:, 2 * c:2 * c + 1], OP.mult, OP.add),
                    reads=("gth", "hall"), writes=("hall",)))
            vself(S.emit("vector", lambda px: px.tensor_tensor(
                hsel[:, :], hall[:, :], sel_s[:, :], OP.mult),
                reads=("hall", "sel"), writes=("hsel",)))
            vself(S.emit("vector", lambda px: px.tensor_tensor(
                hin[:, :], hsel[:, 0:1], hsel[:, 1:2], OP.add),
                reads=("hsel",), writes=("hin",)))
            for c in range(2, NCORES):
                vself(S.emit("vector", lambda px, c=c: px.scalar_tensor_tensor(
                    hin[:, :], hsel[:, c:c + 1], 1.0, hin[:, :], OP.mult, OP.add),
                    reads=("hsel", "hin"), writes=("hin",)))

        def s2_load(j):
            p = j % 2
            base2 = j * TT + 3
            dma("gpsimd", convt[p][:, :], convsp[j, :, :], ("spill",),
                (f"convt{p}",))
            dma("sync",
                x0t[p][:, 0:FCH * CV].rearrange("p (f t) -> p f t", f=FCH),
                xin_r[:, :, base2:base2 + CV], (), (f"x0t{p}",))

        def s2_front_a(j):
            p = j % 2
            ct = convt[p]
            ctres = f"convt{p}"
            x0 = x0t[p]
            xres = f"x0t{p}"
            if j <= 1:
                s2_load(j)

            sgb = sg[p]

            def evac(o, pst, bres):
                S.emit("scalar", lambda px, o=o, pst=pst, sgb=sgb: px.activation(
                    sgb[:, o * CV:(o + 1) * CV], pst[:, :CV], AF.Silu),
                    reads=(bres,), writes=(f"sg{p}",))
            mm_phase(lambda kc, c0, c1: w_in[kc * 128:(kc + 1) * 128,
                                             DI + c0:DI + c1],
                     FCH, lambda kc, x0=x0: x0[:, kc * CV:(kc + 1) * CV], xres,
                     ICH, CV, evac)

        def s2_front_b(j):
            p = j % 2
            ct = convt[p]
            ctres = f"convt{p}"
            S.emit("tensor", lambda px, j=j: px.matmul(
                ps[7][0:1, :CV], hin[:, :], Mall[:, j * CV:(j + 1) * CV],
                start=True, stop=True),
                reads=("hin", "Mall"), writes=("ps7",))
            S.emit("vector", lambda px, j=j: px.tensor_tensor(
                ysb[:, :], ps[7][0:1, :CV], ylall[:, j * CV:(j + 1) * CV], OP.add),
                reads=("ps7", "ylall"), writes=("ysb",))
            S.emit("tensor", lambda px: px.matmul(
                ps[7][:, :CV], o1x128[:, :], ysb[:, :], start=True, stop=True),
                reads=("ysb", "ones"), writes=("ps7",))
            for c in range(ICH):
                acc = accA if c % 2 == 0 else accB
                acr = "accA" if c % 2 == 0 else "accB"
                S.emit("vector", lambda px, c=c, acc=acc, ct=ct:
                       px.scalar_tensor_tensor(
                           acc[:, :CV], ct[:, c * CV:(c + 1) * CV],
                           dpk_s[:, c:c + 1], ps[7][:, :CV], OP.mult, OP.add),
                       reads=(ctres, "dpk", "ps7"), writes=(acr,))
                S.emit("vector", lambda px, c=c, acc=acc, ct=ct, sgb=sg[p]:
                       px.tensor_tensor(
                           ct[:, c * CV:(c + 1) * CV], acc[:, :CV],
                           sgb[:, c * CV:(c + 1) * CV], OP.mult),
                       reads=(acr, f"sg{p}"), writes=(ctres,))

        def s2_back(j):
            p = j % 2
            ct = convt[p]
            ctres = f"convt{p}"
            x0 = x0t[p]
            xres = f"x0t{p}"

            def evac_o(o, pst, bres):
                S.emit("vector", lambda px, o=o, pst=pst, x0=x0: px.tensor_tensor(
                    resid[:, o * CV:(o + 1) * CV], x0[:, o * CV:(o + 1) * CV],
                    pst[:, :CV], OP.add),
                    reads=(bres, xres), writes=("resid",))
            mm_phase(lambda kc, c0, c1: w_out[kc * 128:(kc + 1) * 128, c0:c1],
                     ICH, lambda kc, ct=ct: ct[:, kc * CV:(kc + 1) * CV], ctres,
                     FCH, CV, evac_o)
            if j + 2 < NT:
                s2_load(j + 2)
            ln_block(resid, "resid", x1t, "x1t", CV)
            if j == 0:
                for f in range(FCH):
                    S.emit("vector", lambda px, f=f: px.tensor_scalar(
                        x1t[:, f * CV:f * CV + 1], x1t[:, f * CV:f * CV + 1],
                        mf_s[:, 0:1], None, OP.mult),
                        reads=("x1t", "mf"), writes=("x1t",))
            for f in range(FCH):
                S.emit("vector", lambda px, f=f: px.tensor_tensor(
                    dbuf[:, f * TT:(f + 1) * TT],
                    x1t[:, f * CV + 1:f * CV + 1 + TT],
                    x1t[:, f * CV:f * CV + TT], OP.subtract),
                    reads=("x1t",), writes=("dbuf",))
            for f in range(FCH):
                S.emit("vector", lambda px, f=f: px.scalar_tensor_tensor(
                    rbuf[:, f * TT:(f + 1) * TT], dbuf[:, f * TT:(f + 1) * TT],
                    mur_s[:, f:f + 1], x1t[:, f * CV:f * CV + TT],
                    OP.mult, OP.add),
                    reads=("dbuf", "mur", "x1t"), writes=("rbuf",))
            for f in range(FCH):
                S.emit("vector", lambda px, f=f: px.scalar_tensor_tensor(
                    dbuf[:, f * TT:(f + 1) * TT], dbuf[:, f * TT:(f + 1) * TT],
                    muk_s[:, f:f + 1], x1t[:, f * CV:f * CV + TT],
                    OP.mult, OP.add),
                    reads=("dbuf", "muk", "x1t"), writes=("dbuf",))

            def evac_k(o, pst, bres):
                S.emit("scalar", lambda px, o=o, pst=pst: px.activation(
                    xkb[:, o * TT:(o + 1) * TT], pst[:, :TT], AF.Relu),
                    reads=(bres,), writes=("xkb",))
            mm_phase(lambda kc, c0, c1: wkT[kc * 128:(kc + 1) * 128, c0:c1],
                     FCH, lambda kc: dbuf[:, kc * TT:(kc + 1) * TT], "dbuf",
                     FCH, TT, evac_k)
            for o in range(FCH):
                S.emit("scalar", lambda px, o=o: px.activation(
                    xkb[:, o * TT:(o + 1) * TT], xkb[:, o * TT:(o + 1) * TT],
                    AF.Square),
                    reads=("xkb",), writes=("xkb",))

            def evac_r(o, pst, bres):
                S.emit("scalar", lambda px, o=o, pst=pst: px.activation(
                    dbuf[:, o * TT:(o + 1) * TT], pst[:, :TT], AF.Sigmoid),
                    reads=(bres,), writes=("dbuf",))
            mm_phase(lambda kc, c0, c1: wrT[kc * 128:(kc + 1) * 128, c0:c1],
                     FCH, lambda kc: rbuf[:, kc * TT:(kc + 1) * TT], "rbuf",
                     FCH, TT, evac_r)

            def evac_v(o, pst, bres):
                S.emit("vector", lambda px, o=o, pst=pst: px.tensor_tensor(
                    rbuf[:, o * TT:(o + 1) * TT], dbuf[:, o * TT:(o + 1) * TT],
                    pst[:, :TT], OP.mult),
                    reads=(bres, "dbuf"), writes=("rbuf",))
            mm_phase(lambda kc, c0, c1: wvT[kc * 128:(kc + 1) * 128, c0:c1],
                     FCH, lambda kc: xkb[:, kc * TT:(kc + 1) * TT], "xkb",
                     FCH, TT, evac_v)
            for f in range(FCH):
                S.emit("vector", lambda px, f=f: px.tensor_tensor(
                    rbuf[:, f * TT:(f + 1) * TT],
                    x1t[:, f * CV + 1:f * CV + 1 + TT],
                    rbuf[:, f * TT:(f + 1) * TT], OP.add),
                    reads=("x1t", "rbuf"), writes=("rbuf",))
            ln_block(rbuf, "rbuf", dbuf, "dbuf", TT)
            return dma("gpsimd", outT_r[:, :, j * TT:(j + 1) * TT],
                       dbuf.rearrange("p (f t) -> p f t", f=FCH),
                       ("dbuf",), ("outs",))

        last_out = None
        s2_front_a(0)
        s2_front_a(1)
        emit_collective()
        for j in range(NT + 1):
            if j < NT:
                if j >= 2:
                    s2_front_a(j)
                s2_front_b(j)
            if 1 <= j:
                last_out = s2_back(j - 1)

        S.q["gpsimd"].append({"wait": last_out})

        S.finalize()
        sems = {name: E(nc.semaphore(f"sem_{name}")) for name in S.all_sems()}
        with nc.Block() as block:
            block.sync(S.body("sync", sems))
            block.gpsimd(S.body("gpsimd", sems))
            block.tensor(S.body("tensor", sems))
            block.vector(S.body("vector", sems))
            block.scalar(S.body("scalar", sems))
    return nc


def kernel(L0_attn_input_tensor, L0_attn_W_in, L0_attn_conv_w, L0_attn_conv_b,
           L0_attn_W_x, L0_attn_A_log, L0_attn_D_param, L0_attn_W_out,
           L0_ffn_W_r, L0_ffn_W_k, L0_ffn_W_v, L0_ffn_mu_r, L0_ffn_mu_k):
    f32 = np.float32
    x0 = np.asarray(L0_attn_input_tensor, f32)[0]
    xT = np.ascontiguousarray(x0.T)
    xin_full = np.concatenate([np.zeros((DM, 4), f32), xT], axis=1)

    w_in = np.ascontiguousarray(np.asarray(L0_attn_W_in, f32))
    w_out = np.ascontiguousarray(np.asarray(L0_attn_W_out, f32))
    wrT = np.ascontiguousarray(np.asarray(L0_ffn_W_r, f32).T)
    wkT = np.ascontiguousarray(np.asarray(L0_ffn_W_k, f32).T)
    wvT = np.ascontiguousarray(np.asarray(L0_ffn_W_v, f32).T)
    wxp = np.ascontiguousarray(
        np.asarray(L0_attn_W_x, f32).reshape(ICH, 128, 33)
        .transpose(1, 0, 2).reshape(128, ICH * 33))
    dpk = np.ascontiguousarray(np.asarray(L0_attn_D_param, f32).reshape(ICH, 128).T)
    cbk = np.ascontiguousarray(np.asarray(L0_attn_conv_b, f32).reshape(ICH, 128).T)
    cwk = np.ascontiguousarray(
        np.asarray(L0_attn_conv_w, f32).reshape(ICH, 128, 4)
        .transpose(1, 0, 2).reshape(128, ICH * 4))
    murp = np.ascontiguousarray(np.asarray(L0_ffn_mu_r, f32).reshape(FCH, 128).T)
    mukp = np.ascontiguousarray(np.asarray(L0_ffn_mu_k, f32).reshape(FCH, 128).T)
    alog = np.ascontiguousarray(np.asarray(L0_attn_A_log, f32).reshape(NS, 1))

    nc = build_nc()
    in_maps = []
    for k in range(NCORES):
        sel = np.zeros((NS, NCORES), f32)
        if k > 0:
            sel[:, k - 1] = 1.0
        mfv = np.full((128, 1), 0.0 if k == 0 else 1.0, f32)
        xin_c = np.ascontiguousarray(xin_full[:, k * TC:k * TC + TC + 4])
        in_maps.append({
            "xin": xin_c, "w_in": w_in, "w_out": w_out,
            "wrT": wrT, "wkT": wkT, "wvT": wvT, "wxp": wxp,
            "dpk": dpk, "cbk": cbk, "cwk": cwk,
            "murp": murp, "mukp": mukp, "alog": alog,
            "sel8": sel, "mfirst": mfv,
        })
    res = run_bass_kernel_spmd(nc, in_maps, core_ids=list(range(NCORES)))
    out = np.concatenate([np.asarray(res.results[k]["outT"])
                          for k in range(NCORES)], axis=1)
    return np.ascontiguousarray(out.T).reshape(1, TC * NCORES, DM).astype(f32)


# version 15 (final)
# speedup vs baseline: 1.1161x; 1.0030x over previous
import numpy as np
import concourse.bass as bass
import concourse.mybir as mybir
from concourse.bass_utils import run_bass_kernel_spmd
from contextlib import ExitStack

F32 = mybir.dt.float32
AF = mybir.ActivationFunctionType
OP = mybir.AluOpType
AX = mybir.AxisListType

NCORES = 8
TC = 2048
TT = 256
NT = TC // TT
DM = 1024
FCH = DM // 128
DI = 2048
ICH = DI // 128
NS = 16
C1 = TT + 4
CV = TT + 1
EPS = 1e-5


class Sched:
    ENGS = ("sync", "gpsimd", "tensor", "vector", "scalar")

    def __init__(self):
        self.q = {e: [] for e in self.ENGS}
        self.wr = {}
        self.rd = {}
        self.seq = {"__g": 0}

    def emit(self, eng, fn, reads=(), writes=(), dma=False, async_=False,
             sem=None):
        deps = {}
        for r in reads:
            w = self.wr.get(r)
            if w is not None and (w["eng"] != eng or w["async"]):
                deps[id(w)] = w
        for t in writes:
            w = self.wr.get(t)
            if w is not None and (w["eng"] != eng or w["async"]):
                deps[id(w)] = w
            for rr in self.rd.get(t, ()):
                if rr["eng"] != eng or rr["async"]:
                    deps[id(rr)] = rr
        best = {}
        for d in deps.values():
            b = best.get(d["sem"])
            if b is None or d["gseq"] > b["gseq"]:
                best[d["sem"]] = d
        it = {"eng": eng, "fn": fn, "inc": 16 if dma else (1 if async_ else 0),
              "async": dma or async_, "sem": sem or eng,
              "gseq": self.seq["__g"]}
        self.seq["__g"] = self.seq["__g"] + 1
        for d in best.values():
            if d is it:
                continue
            if d["inc"] == 0:
                d["inc"] = 1
            self.q[eng].append({"wait": d})
        self.q[eng].append(it)
        for r in reads:
            self.rd.setdefault(r, []).append(it)
        for t in writes:
            self.wr[t] = it
            self.rd[t] = []
        return it

    def all_sems(self):
        names = set()
        for e in self.ENGS:
            for it in self.q[e]:
                if "wait" not in it:
                    names.add(it["sem"])
        return sorted(names)

    def finalize(self):
        cnt = {}
        for e in self.ENGS:
            for it in self.q[e]:
                if "wait" not in it:
                    it["ord"] = None
        allitems = []
        for e in self.ENGS:
            allitems.extend(it for it in self.q[e] if "wait" not in it)
        allitems.sort(key=lambda it: it["gseq"])
        for it in allitems:
            c = cnt.get(it["sem"], 0) + it["inc"]
            cnt[it["sem"]] = c
            it["cnt"] = c

    def body(self, eng, sems):
        items = self.q[eng]

        def run(px):
            hi = {}
            for it in items:
                if "wait" in it:
                    d = it["wait"]
                    v = d["cnt"]
                    ts = d["sem"]
                    if v > hi.get(ts, 0):
                        hi[ts] = v
                        px.wait_ge(sems[ts], v)
                else:
                    h = it["fn"](px)
                    if it["inc"]:
                        h.then_inc(sems[it["sem"]], it["inc"])
        return run


def build_nc():
    nc = bass.Bass("TRN2", target_bir_lowering=False, debug=False,
                   detect_race_conditions=False)

    xin = nc.dram_tensor("xin", [DM, TC + 4], F32, kind="ExternalInput").ap()
    w_in = nc.dram_tensor("w_in", [DM, 2 * DI], F32, kind="ExternalInput").ap()
    w_out = nc.dram_tensor("w_out", [DI, DM], F32, kind="ExternalInput").ap()
    wrT = nc.dram_tensor("wrT", [DM, DM], F32, kind="ExternalInput").ap()
    wkT = nc.dram_tensor("wkT", [DM, DM], F32, kind="ExternalInput").ap()
    wvT = nc.dram_tensor("wvT", [DM, DM], F32, kind="ExternalInput").ap()
    wxp = nc.dram_tensor("wxp", [128, ICH * 33], F32, kind="ExternalInput").ap()
    dpk = nc.dram_tensor("dpk", [128, ICH], F32, kind="ExternalInput").ap()
    cbk = nc.dram_tensor("cbk", [128, ICH], F32, kind="ExternalInput").ap()
    cwk = nc.dram_tensor("cwk", [128, ICH * 4], F32, kind="ExternalInput").ap()
    murp = nc.dram_tensor("murp", [128, FCH], F32, kind="ExternalInput").ap()
    mukp = nc.dram_tensor("mukp", [128, FCH], F32, kind="ExternalInput").ap()
    alog = nc.dram_tensor("alog", [NS, 1], F32, kind="ExternalInput").ap()
    sel8 = nc.dram_tensor("sel8", [NS, NCORES], F32, kind="ExternalInput").ap()
    mfirst = nc.dram_tensor("mfirst", [128, 1], F32, kind="ExternalInput").ap()
    outT = nc.dram_tensor("outT", [DM, TC], F32, kind="ExternalOutput").ap()

    convsp = nc.dram_tensor("convsp", [NT, 128, ICH * CV], F32)
    agin = nc.dram_tensor("agin", [1, 2 * NS], F32)
    agout = nc.dram_tensor("agout", [NCORES, 2 * NS], F32)

    xin_r = xin.rearrange("(f p) t -> p f t", p=128)
    outT_r = outT.rearrange("(f p) t -> p f t", p=128)

    S = Sched()
    with ExitStack() as ctx:
        E = ctx.enter_context
        x0t = [E(nc.sbuf_tensor(f"x0t{i}", [128, FCH * C1], F32)) for i in (0, 1)]
        convin = E(nc.sbuf_tensor("convin", [128, ICH * C1], F32))
        convt = [E(nc.sbuf_tensor(f"convt{i}", [128, ICH * CV], F32)) for i in (0, 1)]
        sg = [E(nc.sbuf_tensor(f"sg{i}", [128, ICH * CV], F32)) for i in (0, 1)]
        accA = E(nc.sbuf_tensor("accA", [128, CV], F32))
        accB = E(nc.sbuf_tensor("accB", [128, CV], F32))
        xdbl = E(nc.sbuf_tensor("xdbl", [33, CV], F32))
        bcs = E(nc.sbuf_tensor("bcs", [NS, 2 * CV], F32))
        d16 = E(nc.sbuf_tensor("d16", [NS, CV], F32))
        dAe = E(nc.sbuf_tensor("dAe", [NS, CV], F32))
        dBx = E(nc.sbuf_tensor("dBx", [NS, CV], F32))
        zer = E(nc.sbuf_tensor("zer", [NS, CV], F32))
        gb = E(nc.sbuf_tensor("gb", [NS, CV], F32))
        pb = E(nc.sbuf_tensor("pb", [NS, CV], F32))
        hb = E(nc.sbuf_tensor("hb", [NS, CV], F32))
        ycb = E(nc.sbuf_tensor("ycb", [NS, CV], F32))
        mtb = E(nc.sbuf_tensor("mtb", [NS, CV], F32))
        Mall = E(nc.sbuf_tensor("Mall", [NS, NT * CV], F32))
        ylall = E(nc.sbuf_tensor("ylall", [1, NT * CV], F32))
        Lb = E(nc.sbuf_tensor("Lb", [NS, NT + 1], F32))
        Fb = E(nc.sbuf_tensor("Fb", [NS, NT + 1], F32))
        geb = E(nc.sbuf_tensor("geb", [NS, 2], F32))
        gth = E(nc.sbuf_tensor("gth", [NS, 2 * NCORES], F32))
        hall = E(nc.sbuf_tensor("hall", [NS, NCORES], F32))
        hsel = E(nc.sbuf_tensor("hsel", [NS, NCORES], F32))
        hin = E(nc.sbuf_tensor("hin", [NS, 1], F32))
        dtm = E(nc.sbuf_tensor("dtm", [1, CV], F32))
        dsb = E(nc.sbuf_tensor("dsb", [1, CV], F32))
        expa = E(nc.sbuf_tensor("expa", [NS, 1], F32))
        nega = E(nc.sbuf_tensor("nega", [NS, 1], F32))
        resid = E(nc.sbuf_tensor("resid", [128, FCH * CV], F32))
        sqb = E(nc.sbuf_tensor("sqb", [128, FCH * CV], F32))
        x1t = E(nc.sbuf_tensor("x1t", [128, FCH * CV], F32))
        ysb = E(nc.sbuf_tensor("ysb", [1, CV], F32))
        mean = E(nc.sbuf_tensor("mean", [1, CV], F32))
        msq = E(nc.sbuf_tensor("msq", [1, CV], F32))
        var = E(nc.sbuf_tensor("var", [1, CV], F32))
        sdv = E(nc.sbuf_tensor("sdv", [1, CV], F32))
        rstd = E(nc.sbuf_tensor("rstd", [1, CV], F32))
        dbuf = E(nc.sbuf_tensor("dbuf", [128, FCH * TT], F32))
        xkb = E(nc.sbuf_tensor("xkb", [128, FCH * TT], F32))
        rbuf = E(nc.sbuf_tensor("rbuf", [128, FCH * TT], F32))
        NWB = 12
        w_ab = [E(nc.sbuf_tensor(f"w_ab{i}", [128, 384], F32)) for i in range(NWB)]
        wx_s = E(nc.sbuf_tensor("wx_s", [128, ICH * 33], F32))
        dpk_s = E(nc.sbuf_tensor("dpk_s", [128, ICH], F32))
        cbk_s = E(nc.sbuf_tensor("cbk_s", [128, ICH], F32))
        cwk_s = E(nc.sbuf_tensor("cwk_s", [128, ICH * 4], F32))
        mur_s = E(nc.sbuf_tensor("mur_s", [128, FCH], F32))
        muk_s = E(nc.sbuf_tensor("muk_s", [128, FCH], F32))
        alog_s = E(nc.sbuf_tensor("alog_s", [NS, 1], F32))
        sel_s = E(nc.sbuf_tensor("sel_s", [NS, NCORES], F32))
        mf_s = E(nc.sbuf_tensor("mf_s", [128, 1], F32))
        ones128 = E(nc.sbuf_tensor("ones128", [128, 1], F32))
        o1x128 = E(nc.sbuf_tensor("o1x128", [1, 128], F32))
        ones16 = E(nc.sbuf_tensor("ones16", [NS, 1], F32))
        o1x16 = E(nc.sbuf_tensor("o1x16", [1, NS], F32))
        ps = [E(nc.psum_tensor(f"ps{i}", [128, 512], F32)) for i in range(8)]

        def dma(eng, out, in_, reads, writes, slow=False):
            return S.emit(
                eng,
                lambda px, out=out, in_=in_, slow=slow: px.dma_start(
                    out=out, in_=in_, allow_slow_non_contiguous=slow),
                reads=reads, writes=writes, dma=True, sem="d_" + writes[0])

        wpar = [0]
        rpar = [0]

        def mm_phase(wap_fn, kcount, rhs_fn, rres, ochunks, ncols, evac_fn,
                     post_round=None):
            o0 = 0
            while o0 < ochunks:
                o1 = min(o0 + 3, ochunks)
                base = 3 * rpar[0]
                rpar[0] ^= 1
                for kc in range(kcount):
                    wb = w_ab[wpar[0]]
                    wres = f"w_ab{wpar[0]}"
                    weng = "sync" if wpar[0] % 2 == 0 else "gpsimd"
                    wpar[0] = (wpar[0] + 1) % NWB
                    nw = (o1 - o0) * 128
                    dma(weng, wb[:, 0:nw], wap_fn(kc, o0 * 128, o1 * 128),
                        (), (wres,))
                    rap = rhs_fn(kc)
                    for i in range(o1 - o0):
                        b = base + i
                        S.emit("tensor", lambda px, wb=wb, i=i, kc=kc, b=b,
                               rap=rap, nc_=ncols: px.matmul(
                                   ps[b][:, :nc_], wb[:, i * 128:(i + 1) * 128],
                                   rap, start=(kc == 0), stop=(kc == kcount - 1)),
                               reads=(wres, rres), writes=(f"ps{b}",))
                if post_round is not None:
                    post_round(o0, o1)
                for i in range(o1 - o0):
                    evac_fn(o0 + i, ps[base + i], f"ps{base + i}")
                o0 = o1

        dma("sync", wx_s[:, :], wxp[:, :], (), ("wx",))
        dma("sync", dpk_s[:, :], dpk[:, :], (), ("dpk",))
        dma("sync", cbk_s[:, :], cbk[:, :], (), ("cbk",))
        dma("sync", cwk_s[:, :], cwk[:, :], (), ("cwk",))
        dma("sync", mur_s[:, :], murp[:, :], (), ("mur",))
        dma("sync", muk_s[:, :], mukp[:, :], (), ("muk",))
        dma("sync", alog_s[:, :], alog[:, :], (), ("alog",))
        dma("sync", sel_s[:, :], sel8[:, :], (), ("sel",))
        dma("sync", mf_s[:, :], mfirst[:, :], (), ("mf",))
        for buf, val, res in ((ones128, 1.0, "ones"), (o1x128, 1.0, "ones"),
                              (ones16, 1.0, "ones"), (o1x16, 1.0, "ones"),
                              (zer, 0.0, "zer")):
            S.emit("vector", lambda px, b=buf, v=val: px.memset(b[:, :], v),
                   reads=(), writes=(res,))
        S.emit("vector", lambda px: px.memset(Lb[:, 0:1], 0.0), (), ("Lb",))
        S.emit("vector", lambda px: px.memset(Fb[:, 0:1], 1.0), (), ("Fb",))
        S.emit("scalar", lambda px: px.activation(expa[:, :], alog_s[:, :], AF.Exp),
               reads=("alog",), writes=("expa",))
        S.emit("vector", lambda px: px.tensor_scalar(
            nega[:, :], expa[:, :], -1.0, None, OP.mult),
            reads=("expa",), writes=("nega",))

        def ln_block(src, sres, dst, dres, cols):
            for f in range(FCH):
                S.emit("scalar", lambda px, f=f: px.activation(
                    sqb[:, f * CV:f * CV + cols],
                    src[:, f * cols:f * cols + cols], AF.Square),
                    reads=(sres,), writes=("sqb",))
            for f in range(FCH):
                S.emit("tensor", lambda px, f=f: px.matmul(
                    ps[6][0:1, :cols], ones128[:, :],
                    src[:, f * cols:f * cols + cols],
                    start=(f == 0), stop=(f == FCH - 1)),
                    reads=(sres, "ones"), writes=("ps6",))
            S.emit("vector", lambda px: px.tensor_scalar(
                mean[:, :cols], ps[6][0:1, :cols], 1.0 / DM, None, OP.mult),
                reads=("ps6",), writes=("mean",))
            for f in range(FCH):
                S.emit("tensor", lambda px, f=f: px.matmul(
                    ps[7][0:1, :cols], ones128[:, :],
                    sqb[:, f * CV:f * CV + cols],
                    start=(f == 0), stop=(f == FCH - 1)),
                    reads=("sqb", "ones"), writes=("ps7",))
            S.emit("vector", lambda px: px.tensor_tensor(
                msq[:, :cols], mean[:, :cols], mean[:, :cols], OP.mult),
                reads=("mean",), writes=("msq",))
            S.emit("vector", lambda px: px.tensor_scalar(
                msq[:, :cols], msq[:, :cols], EPS, None, OP.subtract),
                reads=("msq",), writes=("msq",))
            S.emit("vector", lambda px: px.scalar_tensor_tensor(
                var[:, :cols], ps[7][0:1, :cols], 1.0 / DM, msq[:, :cols],
                OP.mult, OP.subtract),
                reads=("ps7", "msq"), writes=("var",))
            S.emit("scalar", lambda px: px.activation(
                sdv[:, :cols], var[:, :cols], AF.Sqrt),
                reads=("var",), writes=("sdv",))
            S.emit("vector", lambda px: px.reciprocal(
                rstd[:, :cols], sdv[:, :cols]),
                reads=("sdv",), writes=("rstd",))
            S.emit("tensor", lambda px: px.matmul(
                ps[6][:, :cols], o1x128[:, :], mean[:, :cols],
                start=True, stop=True),
                reads=("mean", "ones"), writes=("ps6",))
            for f in range(FCH):
                S.emit("vector", lambda px, f=f: px.tensor_tensor(
                    dst[:, f * cols:f * cols + cols],
                    src[:, f * cols:f * cols + cols],
                    ps[6][:, :cols], OP.subtract),
                    reads=(sres, "ps6"), writes=(dres,))
            S.emit("tensor", lambda px: px.matmul(
                ps[7][:, :cols], o1x128[:, :], rstd[:, :cols],
                start=True, stop=True),
                reads=("rstd", "ones"), writes=("ps7",))
            for f in range(FCH):
                S.emit("vector", lambda px, f=f: px.tensor_tensor(
                    dst[:, f * cols:f * cols + cols],
                    dst[:, f * cols:f * cols + cols],
                    ps[7][:, :cols], OP.mult),
                    reads=(dres, "ps7"), writes=(dres,))

        def conv_chunk(j, c):
            p = j % 2
            ct = convt[p]
            ctres = f"convt{p}"
            bi = c * C1
            acc = accA if c % 2 == 0 else accB
            acr = "accA" if c % 2 == 0 else "accB"
            S.emit("vector", lambda px, c=c, bi=bi, acc=acc: px.tensor_scalar(
                acc[:, :CV], convin[:, bi:bi + CV],
                cwk_s[:, c * 4:c * 4 + 1], cbk_s[:, c:c + 1],
                OP.mult, OP.add),
                reads=("convin", "cwk", "cbk"), writes=(acr,))
            for k in range(1, 4):
                S.emit("vector", lambda px, c=c, bi=bi, k=k, acc=acc:
                       px.scalar_tensor_tensor(
                           acc[:, :CV], convin[:, bi + k:bi + k + CV],
                           cwk_s[:, c * 4 + k:c * 4 + k + 1], acc[:, :CV],
                           OP.mult, OP.add),
                       reads=("convin", "cwk", acr), writes=(acr,))
            S.emit("scalar", lambda px, c=c, acc=acc, ct=ct: px.activation(
                ct[:, c * CV:(c + 1) * CV], acc[:, :CV], AF.Silu),
                reads=(acr,), writes=(ctres,))
            if c == ICH - 1:
                dma("gpsimd", convsp[j, :, :], ct[:, :], (ctres,), ("spill",))

        def s1_load(j):
            p = j % 2
            dma("sync", x0t[p].rearrange("p (f t) -> p f t", f=FCH),
                xin_r[:, :, j * TT:j * TT + C1], (), (f"x0t{p}",))

        def s1_xzh1(j):
            p = j % 2
            x0 = x0t[p]
            xres = f"x0t{p}"

            def post(o0, o1):
                if j >= 1:
                    for c in range(o0, o1):
                        conv_chunk(j - 1, c)

            def evac(o, pst, bres):
                S.emit("vector", lambda px, o=o, pst=pst: px.tensor_copy(
                    convin[:, o * C1:(o + 1) * C1], pst[:, :C1]),
                    reads=(bres,), writes=("convin",))
            mm_phase(lambda kc, c0, c1: w_in[kc * 128:(kc + 1) * 128, c0:c1],
                     FCH, lambda kc: x0[:, kc * C1:(kc + 1) * C1], xres,
                     ICH, C1, evac, post_round=post)

        def s1_wx(j):
            p = j % 2
            ct = convt[p]
            ctres = f"convt{p}"
            for kc in range(ICH):
                S.emit("tensor", lambda px, kc=kc, ct=ct: px.matmul(
                    ps[6][0:33, :CV], wx_s[:, kc * 33:(kc + 1) * 33],
                    ct[:, kc * CV:(kc + 1) * CV],
                    start=(kc == 0), stop=(kc == ICH - 1)),
                    reads=("wx", ctres), writes=("ps6",))
            S.emit("vector", lambda px: px.tensor_copy(xdbl[:, :], ps[6][0:33, :CV]),
                   reads=("ps6",), writes=("xdbl",))
            dma("gpsimd", bcs[:, 0:CV], xdbl[1:17, :], ("xdbl",), ("bcs",))
            dma("gpsimd", bcs[:, CV:2 * CV], xdbl[17:33, :], ("xdbl",), ("bcs",))
            S.emit("scalar", lambda px: px.activation(dtm[:, :], xdbl[0:1, :], AF.Exp),
                   reads=("xdbl",), writes=("dtm",))
            S.emit("scalar", lambda px: px.activation(dsb[:, :], dtm[:, :], AF.Ln,
                                                      bias=1.0),
                   reads=("dtm",), writes=("dsb",))
            S.emit("tensor", lambda px: px.matmul(
                ps[7][0:NS, :CV], o1x16[:, :], dsb[:, :], start=True, stop=True),
                reads=("dsb", "ones"), writes=("ps7",))
            S.emit("vector", lambda px: px.tensor_copy(d16[:, :], ps[7][0:NS, :CV]),
                   reads=("ps7",), writes=("d16",))

        def s1_scan(j):
            p = j % 2
            ct = convt[p]
            ctres = f"convt{p}"
            S.emit("vector", lambda px: px.tensor_scalar(
                dAe[:, :], d16[:, :], nega[:, 0:1], None, OP.mult),
                reads=("d16", "nega"), writes=("dAe",))
            S.emit("scalar", lambda px: px.activation(dAe[:, :], dAe[:, :], AF.Exp),
                   reads=("dAe",), writes=("dAe",))
            S.emit("vector", lambda px, ct=ct: px.tensor_tensor(
                dBx[:, :], bcs[:, 0:CV], ct[0:NS, 0:CV], OP.mult),
                reads=("bcs", ctres), writes=("dBx",))
            S.emit("vector", lambda px: px.tensor_tensor(
                dBx[:, :], dBx[:, :], d16[:, :], OP.mult),
                reads=("dBx", "d16"), writes=("dBx",))
            S.emit("vector", lambda px: px.memset(dAe[:, 0:1], 1.0), ("dAe",), ("dAe",))
            S.emit("vector", lambda px: px.memset(dBx[:, 0:1], 0.0), ("dBx",), ("dBx",))
            S.emit("vector", lambda px: px.tensor_tensor_scan(
                gb[:, :], dAe[:, :], dBx[:, :], 0.0, OP.mult, OP.add),
                reads=("dAe", "dBx"), writes=("gb",))
            S.emit("vector", lambda px: px.tensor_tensor_scan(
                pb[:, :], dAe[:, :], zer[:, :], 1.0, OP.mult, OP.add),
                reads=("dAe", "zer"), writes=("pb",))
            S.emit("vector", lambda px, j=j: px.scalar_tensor_tensor(
                hb[:, :], pb[:, :], Lb[:, j:j + 1], gb[:, :], OP.mult, OP.add),
                reads=("pb", "Lb", "gb"), writes=("hb",))
            S.emit("vector", lambda px: px.tensor_tensor(
                ycb[:, :], hb[:, :], bcs[:, CV:2 * CV], OP.mult),
                reads=("hb", "bcs"), writes=("ycb",))
            S.emit("vector", lambda px: px.tensor_tensor(
                mtb[:, :], bcs[:, CV:2 * CV], pb[:, :], OP.mult),
                reads=("bcs", "pb"), writes=("mtb",))
            S.emit("vector", lambda px, j=j: px.tensor_scalar(
                Mall[:, j * CV:(j + 1) * CV], mtb[:, :], Fb[:, j:j + 1], None,
                OP.mult),
                reads=("mtb", "Fb"), writes=("Mall",))
            S.emit("vector", lambda px, j=j: px.scalar_tensor_tensor(
                Lb[:, j + 1:j + 2], pb[:, CV - 1:CV], Lb[:, j:j + 1],
                gb[:, CV - 1:CV], OP.mult, OP.add),
                reads=("pb", "gb", "Lb"), writes=("Lb",))
            S.emit("vector", lambda px, j=j: px.tensor_tensor(
                Fb[:, j + 1:j + 2], pb[:, CV - 1:CV], Fb[:, j:j + 1], OP.mult),
                reads=("pb", "Fb"), writes=("Fb",))

        def s1_yloc(j):
            S.emit("tensor", lambda px: px.matmul(
                ps[7][0:1, :CV], ones16[:, :], ycb[:, :], start=True, stop=True),
                reads=("ycb", "ones"), writes=("ps7",))
            S.emit("vector", lambda px, j=j: px.tensor_copy(
                ylall[:, j * CV:(j + 1) * CV], ps[7][0:1, :CV]),
                reads=("ps7",), writes=("ylall",))

        s1_load(0)
        for j in range(NT + 2):
            if j < NT:
                s1_xzh1(j)
                if j + 1 < NT:
                    s1_load(j + 1)
            elif j == NT:
                for c in range(ICH):
                    conv_chunk(NT - 1, c)
            if 2 <= j:
                s1_yloc(j - 2)
            if 1 <= j <= NT:
                s1_wx(j - 1)
                s1_scan(j - 1)

        def emit_collective():
            S.emit("vector", lambda px: px.tensor_copy(geb[:, 0:1], Lb[:, NT:NT + 1]),
                   reads=("Lb",), writes=("geb",))
            S.emit("vector", lambda px: px.tensor_copy(geb[:, 1:2], Fb[:, NT:NT + 1]),
                   reads=("Fb",), writes=("geb",))
            dma("gpsimd", agin[:, :], geb[:, :], ("geb",), ("agin",))
            S.emit("gpsimd", lambda px: px.collective_compute(
                "AllGather", OP.bypass, replica_groups=[list(range(NCORES))],
                ins=[agin.ap()], outs=[agout.ap()]),
                reads=("agin",), writes=("agout",), async_=True)
            dma("gpsimd", gth[:, :].rearrange("p (c x) -> p c x", x=2),
                agout.ap().rearrange("c (i x) -> i c x", x=2),
                ("agout",), ("gth",), slow=True)
            def vself(it):
                if it["inc"] == 0:
                    it["inc"] = 1
                S.q["vector"].append({"wait": it})
            vself(S.emit("vector",
                         lambda px: px.tensor_copy(hall[:, 0:1], gth[:, 0:1]),
                         reads=("gth",), writes=("hall",)))
            for c in range(1, NCORES):
                vself(S.emit("vector", lambda px, c=c: px.scalar_tensor_tensor(
                    hall[:, c:c + 1], gth[:, 2 * c + 1:2 * c + 2],
                    hall[:, c - 1:c], gth[:, 2 * c:2 * c + 1], OP.mult, OP.add),
                    reads=("gth", "hall"), writes=("hall",)))
            vself(S.emit("vector", lambda px: px.tensor_tensor(
                hsel[:, :], hall[:, :], sel_s[:, :], OP.mult),
                reads=("hall", "sel"), writes=("hsel",)))
            vself(S.emit("vector", lambda px: px.tensor_tensor(
                hin[:, :], hsel[:, 0:1], hsel[:, 1:2], OP.add),
                reads=("hsel",), writes=("hin",)))
            for c in range(2, NCORES):
                vself(S.emit("vector", lambda px, c=c: px.scalar_tensor_tensor(
                    hin[:, :], hsel[:, c:c + 1], 1.0, hin[:, :], OP.mult, OP.add),
                    reads=("hsel", "hin"), writes=("hin",)))

        def s2_load(j):
            p = j % 2
            base2 = j * TT + 3
            dma("gpsimd", convt[p][:, :], convsp[j, :, :], ("spill",),
                (f"convt{p}",))
            dma("sync",
                x0t[p][:, 0:FCH * CV].rearrange("p (f t) -> p f t", f=FCH),
                xin_r[:, :, base2:base2 + CV], (), (f"x0t{p}",))

        def s2_front_a(j):
            p = j % 2
            ct = convt[p]
            ctres = f"convt{p}"
            x0 = x0t[p]
            xres = f"x0t{p}"
            if j <= 1:
                s2_load(j)

            sgb = sg[p]

            def evac(o, pst, bres):
                S.emit("scalar", lambda px, o=o, pst=pst, sgb=sgb: px.activation(
                    sgb[:, o * CV:(o + 1) * CV], pst[:, :CV], AF.Silu),
                    reads=(bres,), writes=(f"sg{p}",))
            mm_phase(lambda kc, c0, c1: w_in[kc * 128:(kc + 1) * 128,
                                             DI + c0:DI + c1],
                     FCH, lambda kc, x0=x0: x0[:, kc * CV:(kc + 1) * CV], xres,
                     ICH, CV, evac)

        def s2_front_b(j):
            p = j % 2
            ct = convt[p]
            ctres = f"convt{p}"
            S.emit("tensor", lambda px, j=j: px.matmul(
                ps[7][0:1, :CV], hin[:, :], Mall[:, j * CV:(j + 1) * CV],
                start=True, stop=True),
                reads=("hin", "Mall"), writes=("ps7",))
            S.emit("vector", lambda px, j=j: px.tensor_tensor(
                ysb[:, :], ps[7][0:1, :CV], ylall[:, j * CV:(j + 1) * CV], OP.add),
                reads=("ps7", "ylall"), writes=("ysb",))
            S.emit("tensor", lambda px: px.matmul(
                ps[7][:, :CV], o1x128[:, :], ysb[:, :], start=True, stop=True),
                reads=("ysb", "ones"), writes=("ps7",))
            for c in range(ICH):
                acc = accA if c % 2 == 0 else accB
                acr = "accA" if c % 2 == 0 else "accB"
                S.emit("vector", lambda px, c=c, acc=acc, ct=ct:
                       px.scalar_tensor_tensor(
                           acc[:, :CV], ct[:, c * CV:(c + 1) * CV],
                           dpk_s[:, c:c + 1], ps[7][:, :CV], OP.mult, OP.add),
                       reads=(ctres, "dpk", "ps7"), writes=(acr,))
                S.emit("vector", lambda px, c=c, acc=acc, ct=ct, sgb=sg[p]:
                       px.tensor_tensor(
                           ct[:, c * CV:(c + 1) * CV], acc[:, :CV],
                           sgb[:, c * CV:(c + 1) * CV], OP.mult),
                       reads=(acr, f"sg{p}"), writes=(ctres,))

        def s2_back(j):
            p = j % 2
            ct = convt[p]
            ctres = f"convt{p}"
            x0 = x0t[p]
            xres = f"x0t{p}"

            def evac_o(o, pst, bres):
                S.emit("vector", lambda px, o=o, pst=pst, x0=x0: px.tensor_tensor(
                    resid[:, o * CV:(o + 1) * CV], x0[:, o * CV:(o + 1) * CV],
                    pst[:, :CV], OP.add),
                    reads=(bres, xres), writes=("resid",))
            mm_phase(lambda kc, c0, c1: w_out[kc * 128:(kc + 1) * 128, c0:c1],
                     ICH, lambda kc, ct=ct: ct[:, kc * CV:(kc + 1) * CV], ctres,
                     FCH, CV, evac_o)
            if j + 2 < NT:
                s2_load(j + 2)
            ln_block(resid, "resid", x1t, "x1t", CV)
            if j == 0:
                for f in range(FCH):
                    S.emit("vector", lambda px, f=f: px.tensor_scalar(
                        x1t[:, f * CV:f * CV + 1], x1t[:, f * CV:f * CV + 1],
                        mf_s[:, 0:1], None, OP.mult),
                        reads=("x1t", "mf"), writes=("x1t",))
            for f in range(FCH):
                S.emit("vector", lambda px, f=f: px.tensor_tensor(
                    dbuf[:, f * TT:(f + 1) * TT],
                    x1t[:, f * CV + 1:f * CV + 1 + TT],
                    x1t[:, f * CV:f * CV + TT], OP.subtract),
                    reads=("x1t",), writes=("dbuf",))
            for f in range(FCH):
                S.emit("vector", lambda px, f=f: px.scalar_tensor_tensor(
                    rbuf[:, f * TT:(f + 1) * TT], dbuf[:, f * TT:(f + 1) * TT],
                    mur_s[:, f:f + 1], x1t[:, f * CV:f * CV + TT],
                    OP.mult, OP.add),
                    reads=("dbuf", "mur", "x1t"), writes=("rbuf",))
            for f in range(FCH):
                S.emit("vector", lambda px, f=f: px.scalar_tensor_tensor(
                    dbuf[:, f * TT:(f + 1) * TT], dbuf[:, f * TT:(f + 1) * TT],
                    muk_s[:, f:f + 1], x1t[:, f * CV:f * CV + TT],
                    OP.mult, OP.add),
                    reads=("dbuf", "muk", "x1t"), writes=("dbuf",))

            def evac_k(o, pst, bres):
                S.emit("scalar", lambda px, o=o, pst=pst: px.activation(
                    xkb[:, o * TT:(o + 1) * TT], pst[:, :TT], AF.Relu),
                    reads=(bres,), writes=("xkb",))
            mm_phase(lambda kc, c0, c1: wkT[kc * 128:(kc + 1) * 128, c0:c1],
                     FCH, lambda kc: dbuf[:, kc * TT:(kc + 1) * TT], "dbuf",
                     FCH, TT, evac_k)
            for o in range(FCH):
                S.emit("scalar", lambda px, o=o: px.activation(
                    xkb[:, o * TT:(o + 1) * TT], xkb[:, o * TT:(o + 1) * TT],
                    AF.Square),
                    reads=("xkb",), writes=("xkb",))

            def evac_r(o, pst, bres):
                S.emit("scalar", lambda px, o=o, pst=pst: px.activation(
                    dbuf[:, o * TT:(o + 1) * TT], pst[:, :TT], AF.Sigmoid),
                    reads=(bres,), writes=("dbuf",))
            mm_phase(lambda kc, c0, c1: wrT[kc * 128:(kc + 1) * 128, c0:c1],
                     FCH, lambda kc: rbuf[:, kc * TT:(kc + 1) * TT], "rbuf",
                     FCH, TT, evac_r)

            def evac_v(o, pst, bres):
                S.emit("vector", lambda px, o=o, pst=pst: px.tensor_tensor(
                    rbuf[:, o * TT:(o + 1) * TT], dbuf[:, o * TT:(o + 1) * TT],
                    pst[:, :TT], OP.mult),
                    reads=(bres, "dbuf"), writes=("rbuf",))
            mm_phase(lambda kc, c0, c1: wvT[kc * 128:(kc + 1) * 128, c0:c1],
                     FCH, lambda kc: xkb[:, kc * TT:(kc + 1) * TT], "xkb",
                     FCH, TT, evac_v)
            for f in range(FCH):
                S.emit("vector", lambda px, f=f: px.tensor_tensor(
                    rbuf[:, f * TT:(f + 1) * TT],
                    x1t[:, f * CV + 1:f * CV + 1 + TT],
                    rbuf[:, f * TT:(f + 1) * TT], OP.add),
                    reads=("x1t", "rbuf"), writes=("rbuf",))
            ln_block(rbuf, "rbuf", dbuf, "dbuf", TT)
            return dma("gpsimd", outT_r[:, :, j * TT:(j + 1) * TT],
                       dbuf.rearrange("p (f t) -> p f t", f=FCH),
                       ("dbuf",), ("outs",))

        last_out = None
        s2_front_a(0)
        s2_front_a(1)
        emit_collective()
        for j in range(NT + 1):
            if j < NT:
                if j >= 2:
                    s2_front_a(j)
                s2_front_b(j)
            if 1 <= j:
                last_out = s2_back(j - 1)

        S.q["gpsimd"].append({"wait": last_out})

        S.finalize()
        sems = {name: E(nc.semaphore(f"sem_{name}")) for name in S.all_sems()}
        with nc.Block() as block:
            block.sync(S.body("sync", sems))
            block.gpsimd(S.body("gpsimd", sems))
            block.tensor(S.body("tensor", sems))
            block.vector(S.body("vector", sems))
            block.scalar(S.body("scalar", sems))
    return nc


def kernel(L0_attn_input_tensor, L0_attn_W_in, L0_attn_conv_w, L0_attn_conv_b,
           L0_attn_W_x, L0_attn_A_log, L0_attn_D_param, L0_attn_W_out,
           L0_ffn_W_r, L0_ffn_W_k, L0_ffn_W_v, L0_ffn_mu_r, L0_ffn_mu_k):
    f32 = np.float32
    x0 = np.asarray(L0_attn_input_tensor, f32)[0]
    xT = np.ascontiguousarray(x0.T)
    xin_full = np.concatenate([np.zeros((DM, 4), f32), xT], axis=1)

    w_in = np.ascontiguousarray(np.asarray(L0_attn_W_in, f32))
    w_out = np.ascontiguousarray(np.asarray(L0_attn_W_out, f32))
    wrT = np.ascontiguousarray(np.asarray(L0_ffn_W_r, f32).T)
    wkT = np.ascontiguousarray(np.asarray(L0_ffn_W_k, f32).T)
    wvT = np.ascontiguousarray(np.asarray(L0_ffn_W_v, f32).T)
    wxp = np.ascontiguousarray(
        np.asarray(L0_attn_W_x, f32).reshape(ICH, 128, 33)
        .transpose(1, 0, 2).reshape(128, ICH * 33))
    dpk = np.ascontiguousarray(np.asarray(L0_attn_D_param, f32).reshape(ICH, 128).T)
    cbk = np.ascontiguousarray(np.asarray(L0_attn_conv_b, f32).reshape(ICH, 128).T)
    cwk = np.ascontiguousarray(
        np.asarray(L0_attn_conv_w, f32).reshape(ICH, 128, 4)
        .transpose(1, 0, 2).reshape(128, ICH * 4))
    murp = np.ascontiguousarray(np.asarray(L0_ffn_mu_r, f32).reshape(FCH, 128).T)
    mukp = np.ascontiguousarray(np.asarray(L0_ffn_mu_k, f32).reshape(FCH, 128).T)
    alog = np.ascontiguousarray(np.asarray(L0_attn_A_log, f32).reshape(NS, 1))

    nc = build_nc()
    in_maps = []
    for k in range(NCORES):
        sel = np.zeros((NS, NCORES), f32)
        if k > 0:
            sel[:, k - 1] = 1.0
        mfv = np.full((128, 1), 0.0 if k == 0 else 1.0, f32)
        xin_c = np.ascontiguousarray(xin_full[:, k * TC:k * TC + TC + 4])
        in_maps.append({
            "xin": xin_c, "w_in": w_in, "w_out": w_out,
            "wrT": wrT, "wkT": wkT, "wvT": wvT, "wxp": wxp,
            "dpk": dpk, "cbk": cbk, "cwk": cwk,
            "murp": murp, "mukp": mukp, "alog": alog,
            "sel8": sel, "mfirst": mfv,
        })
    res = run_bass_kernel_spmd(nc, in_maps, core_ids=list(range(NCORES)))
    out = np.concatenate([np.asarray(res.results[k]["outT"])
                          for k in range(NCORES)], axis=1)
    return np.ascontiguousarray(out.T).reshape(1, TC * NCORES, DM).astype(f32)
